# version 05
# baseline (speedup 1.0000x reference)
import numpy as np
import ml_dtypes
import concourse.bass as bass
import concourse.mybir as mybir
from concourse.bass_utils import run_bass_kernel_spmd
from contextlib import ExitStack

F32 = mybir.dt.float32
F32R = mybir.dt.float32r
BF16 = mybir.dt.bfloat16
AF = mybir.ActivationFunctionType
OP = mybir.AluOpType

S = 2
T = 2048
C = 1024
H = 16
HD = 64
FF = 4096
NCH = C // 128
NTT = T // 128
QC = 512
NQC = T // QC
NFH = FF // 128
LNEPS = 1e-5
AEPS = 1e-6
SCALE = 0.125
ESHIFT = -12.0

ENGINES = ("sp", "act", "dve", "pe", "pool")


class Plan:
    def __init__(self):
        self.ops = {e: [] for e in ENGINES}
        self.cnt = {e: 0 for e in ENGINES}

    def op(self, eng, fn, after=(), inc=1):
        waits = {}
        for tok in after:
            if tok is None:
                continue
            e2, v2 = tok
            if v2 > waits.get(e2, 0):
                waits[e2] = v2
        self.cnt[eng] += inc
        self.ops[eng].append((waits, fn, inc))
        return (eng, self.cnt[eng])

    def dma(self, eng, fn, after=()):
        return self.op(eng, fn, after, inc=16)

    def fence(self):
        return [(e, self.cnt[e]) for e in ENGINES if self.cnt[e] > 0]

    def emit(self, nc, block, stack):
        sems = {e: stack.enter_context(nc.semaphore(f"sem_{e}")) for e in ENGINES}
        engmap = {"sp": block.sync, "act": block.scalar, "dve": block.vector,
                  "pe": block.tensor, "pool": block.gpsimd}
        plan = self

        def mk(eng_name):
            def body(E):
                seen = {e: 0 for e in ENGINES}
                for waits, fn, inc in plan.ops[eng_name]:
                    for e2 in sorted(waits):
                        v2 = waits[e2]
                        if v2 > seen[e2]:
                            E.wait_ge(sems[e2], v2)
                            seen[e2] = v2
                    fn(E).then_inc(sems[eng_name], inc)
                if eng_name == "sp":
                    E.wait_ge(sems["sp"], plan.cnt["sp"])
                    if plan.cnt["pool"]:
                        E.wait_ge(sems["pool"], plan.cnt["pool"])
            return body

        for e in ENGINES:
            if plan.ops[e] or e == "sp":
                engmap[e](mk(e))


class Rot:

    def __init__(self, n):
        self.n = n
        self.wr = [None] * n
        self.rd = [[] for _ in range(n)]

    def war(self, i):
        s = i % self.n
        dep = list(self.rd[s])
        if self.wr[s] is not None:
            dep.append(self.wr[s])
        return dep

    def wrote(self, i, tok):
        s = i % self.n
        self.wr[s] = tok
        self.rd[s] = []

    def read(self, i, tok):
        self.rd[i % self.n].append(tok)

    def wtok(self, i):
        return self.wr[i % self.n]


def ts(i, sz):
    return slice(i * sz, (i + 1) * sz)


def lowp(nc, f):
    def g(E):
        with nc.allow_low_precision(reason="fp32r broadcast path"):
            return f(E)
    return g


def build_program(n_sub=4, dbg=False):
    nc = bass.Bass("TRN2", target_bir_lowering=False, debug=False)
    NT = S * T

    xin = nc.dram_tensor("xin", [NT, C], F32, kind="ExternalInput").ap()
    xout = nc.dram_tensor("xout", [NT, C], F32, kind="ExternalOutput").ap()
    wqkv_a = nc.dram_tensor("wqkv_a", [3 * C, C], F32, kind="ExternalInput").ap()
    wproj_a = nc.dram_tensor("wproj_a", [C, C], F32, kind="ExternalInput").ap()
    bproj_a = nc.dram_tensor("bproj_a", [C], F32, kind="ExternalInput").ap()
    wqkv_b = nc.dram_tensor("wqkv_b", [3 * C, C], F32, kind="ExternalInput").ap()
    wproj_b = nc.dram_tensor("wproj_b", [C, C], F32, kind="ExternalInput").ap()
    bproj_b = nc.dram_tensor("bproj_b", [C], F32, kind="ExternalInput").ap()
    w1d = nc.dram_tensor("w1", [C, FF], F32, kind="ExternalInput").ap()
    b1d = nc.dram_tensor("b1", [FF], F32, kind="ExternalInput").ap()
    w2d = nc.dram_tensor("w2", [FF, C], F32, kind="ExternalInput").ap()
    b2d = nc.dram_tensor("b2", [C], F32, kind="ExternalInput").ap()
    ident_in = nc.dram_tensor("ident", [128, 128], BF16, kind="ExternalInput").ap()

    xs = [nc.dram_tensor(f"xs{i}", [NT, C], F32, kind="Internal").ap() for i in range(2)]
    if dbg:
        dbg_yT = nc.dram_tensor("dbg_yT", [128, NCH * QC], BF16, kind="ExternalOutput").ap()
        dbg_q = nc.dram_tensor("dbg_q", [128, NCH * T], BF16, kind="ExternalOutput").ap()
        dbg_kv = nc.dram_tensor("dbg_kv", [128, 16 * 65], F32, kind="ExternalOutput").ap()
        dbg_outf = nc.dram_tensor("dbg_outf", [128, NCH * QC], BF16, kind="ExternalOutput").ap()
        dbg_prj = nc.dram_tensor("dbg_prj", [128, 4 * QC], F32, kind="ExternalOutput").ap()
        dbg_xnew = nc.dram_tensor("dbg_xnew", [128, C], F32, kind="ExternalOutput").ap()
        dbg_ptr = nc.dram_tensor("dbg_ptr", [128, QC], F32, kind="ExternalOutput").ap()
        dbg_yT2 = nc.dram_tensor("dbg_yT2", [128, NCH * QC], BF16, kind="ExternalOutput").ap()
        dbg_h = nc.dram_tensor("dbg_h", [128, NFH * QC], BF16, kind="ExternalOutput").ap()
        dbg_osb = nc.dram_tensor("dbg_osb", [128, 2 * QC], F32, kind="ExternalOutput").ap()
        dbg_outfb = nc.dram_tensor("dbg_outfb", [128, NCH * QC], BF16, kind="ExternalOutput").ap()
        dbg_expb = nc.dram_tensor("dbg_expb", [128, QC], BF16, kind="ExternalOutput").ap()
    qspill = nc.dram_tensor("qspill", [C, T], BF16, kind="Internal").ap()
    kspill = nc.dram_tensor("kspill", [C, T], BF16, kind="Internal").ap()

    def xsrc(sub):
        return xin if sub == 0 else xs[(sub - 1) % 2]

    def xdst(sub):
        return xout if sub == n_sub - 1 else xs[sub % 2]

    p = Plan()
    xstore = {}

    def src_dep(sub, gt):
        if sub == 0:
            return []
        return [xstore.get((sub - 1, gt, 0)), xstore.get((sub - 1, gt, 1)),
                xstore.get((sub - 1, gt, "f0")), xstore.get((sub - 1, gt, "f1")),
                xstore.get((sub - 1, gt, "f2")), xstore.get((sub - 1, gt, "f3")),
                xstore.get((sub - 1, gt))]

    with ExitStack() as stack:
        e = stack.enter_context

        ident = e(nc.sbuf_tensor("identb", [128, 128], BF16))
        ident_f = e(nc.sbuf_tensor("identf", [128, 128], F32))
        ones_r = e(nc.sbuf_tensor("ones_r", [128, 64], F32R))
        consts = e(nc.sbuf_tensor("consts", [128, 8], F32))
        bias_pa = e(nc.sbuf_tensor("bias_pa", [128, NCH], F32))
        bias_pb = e(nc.sbuf_tensor("bias_pb", [128, NCH], F32))
        bias_1 = e(nc.sbuf_tensor("bias_1", [128, NFH], F32))
        bias_2 = e(nc.sbuf_tensor("bias_2", [128, NCH], F32))
        stats = e(nc.sbuf_tensor("stats", [128, 2 * 12], F32))
        aggr = e(nc.sbuf_tensor("aggr", [128, 2 * 4], F32))
        lnsc = e(nc.sbuf_tensor("lnsc", [128, 2 * 32], F32))
        xq = e(nc.sbuf_tensor("xq", [128, 4 * C], F32))
        ybuf = e(nc.sbuf_tensor("ybuf", [128, 2 * C], BF16))
        xnew = e(nc.sbuf_tensor("xnew", [128, 2 * C], F32))

        tok_ident = p.dma("sp", lambda E: E.dma_start(out=ident[:], in_=ident_in))
        tok_identf = p.op("dve", lambda E: E.tensor_copy(ident_f[:], ident[:]),
                          after=[tok_ident])
        p.op("dve", lambda E: E.memset(consts[:, 0:1], LNEPS))
        p.op("dve", lambda E: E.memset(consts[:, 1:2], AEPS))
        p.op("dve", lambda E: E.memset(consts[:, 2:3], ESHIFT))
        tok_c3 = p.op("dve", lambda E: E.memset(consts[:, 3:4], 1.0))
        p.op("dve", lambda E: E.memset(consts[:, 4:5], 0.5))
        p.op("dve", lambda E: E.memset(consts[:, 5:6], -0.5))
        p.op("dve", lambda E: E.memset(consts[:, 6:7], -1.0))
        tok_ones = p.op("dve", lowp(nc, lambda E: E.tensor_copy(
            ones_r[:], bass.AP(consts, 3, [[8, 128], [0, 64]]))), after=[tok_c3])
        tok_consts = tok_ones

        def load_bias(dst, src, n):
            return p.dma("pool", lambda E: E.dma_start(
                out=dst[:, 0:n], in_=bass.AP(src.tensor, 0, [[1, 128], [128, n]]),
                allow_slow_non_contiguous=True))

        tok_bias = [load_bias(bias_pa, bproj_a, NCH), load_bias(bias_pb, bproj_b, NCH),
                    load_bias(bias_1, b1d, NFH), load_bias(bias_2, b2d, NCH)]

        rot_stats = Rot(2)
        rot_xq = Rot(4)
        rot_y = Rot(2)
        rot_xnew = Rot(8)

        def emit_ln_qchunk(sub, src_dram, s_i, qc_i, gi, pLN, rotLN, yT, nyT,
                           rot_yT, extra_dep=()):
            l0 = (gi % 2) * 32
            u4s = []
            tile_info = []
            for tt in range(4):
                ti = gi * 4 + tt
                gt = (s_i * T + qc_i * QC + tt * 128) // 128
                r0 = gt * 128
                sl = ti % 4
                dep = rot_xq.war(ti) + src_dep(sub, gt) + list(extra_dep)
                tl = p.dma("sp", lambda E, sl=sl, r0=r0, src_dram=src_dram: E.dma_start(
                    out=xq[:, ts(sl, C)], in_=src_dram[r0:r0 + 128, :]), after=dep)
                rot_xq.wrote(ti, tl)
                ss = ti % 2
                a0 = ss * 4
                t1 = p.op("dve", lambda E, sl=sl, ss=ss: E.bn_stats(
                    out=stats[:, ss * 12:ss * 12 + 6], in_=xq[:, sl * C:sl * C + 512]),
                    after=[tl] + rot_stats.war(ti))
                t2 = p.op("dve", lambda E, sl=sl, ss=ss: E.bn_stats(
                    out=stats[:, ss * 12 + 6:ss * 12 + 12],
                    in_=xq[:, sl * C + 512:sl * C + 1024]), after=[tl])
                t3a = p.op("dve", lambda E, ss=ss, a0=a0: E.bn_aggr(
                    out=aggr[:, a0:a0 + 2], in_=stats[:, ss * 12:ss * 12 + 6]),
                    after=[t1])
                t3 = p.op("dve", lambda E, ss=ss, a0=a0: E.bn_aggr(
                    out=aggr[:, a0 + 2:a0 + 4], in_=stats[:, ss * 12 + 6:ss * 12 + 12]),
                    after=[t2])
                u1 = p.op("dve", lambda E, a0=a0, l0=l0, tt=tt: E.tensor_sub(
                    out=lnsc[:, l0 + 12 + tt:l0 + 13 + tt], in0=aggr[:, a0:a0 + 1],
                    in1=aggr[:, a0 + 2:a0 + 3]), after=[t3a, t3])
                u2 = p.op("dve", lambda E, l0=l0, tt=tt: E.tensor_mul(
                    out=lnsc[:, l0 + 16 + tt:l0 + 17 + tt],
                    in0=lnsc[:, l0 + 12 + tt:l0 + 13 + tt],
                    in1=lnsc[:, l0 + 12 + tt:l0 + 13 + tt]), after=[u1])
                u3 = p.op("dve", lambda E, a0=a0, l0=l0, tt=tt: E.tensor_add(
                    out=lnsc[:, l0 + 20 + tt:l0 + 21 + tt], in0=aggr[:, a0 + 1:a0 + 2],
                    in1=aggr[:, a0 + 3:a0 + 4]), after=[t3a, t3])
                u4 = p.op("dve", lambda E, l0=l0, tt=tt: E.scalar_tensor_tensor(
                    out=lnsc[:, l0 + tt:l0 + tt + 1],
                    in0=lnsc[:, l0 + 16 + tt:l0 + 17 + tt], scalar=0.5,
                    in1=lnsc[:, l0 + 20 + tt:l0 + 21 + tt],
                    op0=OP.mult, op1=OP.add), after=[u2, u3])
                u5 = p.op("dve", lambda E, a0=a0, l0=l0, tt=tt: E.tensor_scalar(
                    out=lnsc[:, l0 + 4 + tt:l0 + 5 + tt], in0=aggr[:, a0:a0 + 1],
                    scalar1=aggr[:, a0 + 2:a0 + 3], scalar2=0.5,
                    op0=OP.add, op1=OP.mult), after=[t3a, t3])
                rot_stats.wrote(ti, u5)
                u4s.append(u4)
                u4s.append(u5)
                tile_info.append((ti, tt, sl, tl))
            L1 = p.op("act", lambda E, l0=l0: E.activation(
                out=lnsc[:, l0 + 24:l0 + 28], in_=lnsc[:, l0:l0 + 4],
                func=AF.Ln, bias=consts[:, 0:1], scale=consts[:, 4:5]),
                after=u4s + [tok_consts])
            L2 = p.op("act", lambda E, l0=l0: E.activation(
                out=lnsc[:, l0 + 8:l0 + 12], in_=lnsc[:, l0 + 24:l0 + 28],
                func=AF.Exp, scale=consts[:, 5:6]), after=[L1])
            last_ev = None
            for (ti, tt, sl, tl) in tile_info:
                ss = ti % 2
                t7 = p.op("dve", lambda E, sl=sl, ss=ss, l0=l0, tt=tt: E.tensor_scalar(
                    out=ybuf[:, ts(ss, C)], in0=xq[:, ts(sl, C)],
                    scalar1=lnsc[:, l0 + 4 + tt:l0 + 5 + tt],
                    scalar2=lnsc[:, l0 + 8 + tt:l0 + 9 + tt],
                    op0=OP.subtract, op1=OP.mult),
                    after=[L2, tl] + rot_y.war(ti))
                rot_y.wrote(ti, t7)
                rot_xq.read(ti, t7)
                pb = pLN[ti % 2]
                war0 = rotLN.war(ti) + ([last_ev] if last_ev else []) + rot_yT.war(gi)
                tr_last = None
                for cc in range(NCH):
                    tr_last = p.op("pe", lambda E, ss=ss, cc=cc, pb=pb: E.transpose(
                        pb[:, ts(cc, 128)],
                        ybuf[:, ss * C + cc * 128: ss * C + (cc + 1) * 128], ident[:]),
                        after=([t7, tok_ident] + war0 if cc == 0 else [t7]))
                rot_y.read(ti, tr_last)
                tev = p.op("dve", lambda E, tt=tt, gi=gi, pb=pb, yT=yT, nyT=nyT: E.tensor_copy(
                    bass.AP(yT, (gi % nyT) * NCH * QC + tt * 128,
                            [[nyT * NCH * QC, 128], [QC, NCH], [1, 128]]), pb[:]),
                    after=[tr_last])
                rotLN.wrote(ti, tev)
                last_ev = tev
            rot_yT.wrote(gi, last_ev)
            x_toks = [inf[3] for inf in tile_info]
            return last_ev, x_toks, gi % nyT

        def emit_wT(wdram, R, wT, wnat, rot_wnat, pLN, rotLN, base, extra=()):
            toks = []
            for rb in range(R // 128):
                tl = p.dma("pool", lambda E, rb=rb, wdram=wdram, base=base: E.dma_start(
                    out=wnat[:, ts(rb % 2, C)],
                    in_=wdram[base + rb * 128: base + (rb + 1) * 128, :]),
                    after=rot_wnat.war(rb) + list(extra))
                rot_wnat.wrote(rb, tl)
                pb = pLN[rb % 2]
                tr_last = None
                for cc in range(NCH):
                    tr_last = p.op("pe", lambda E, rb=rb, cc=cc, pb=pb: E.transpose(
                        pb[:, ts(cc, 128)],
                        wnat[:, (rb % 2) * C + cc * 128:(rb % 2) * C + (cc + 1) * 128],
                        ident[:]),
                        after=([tl, tok_ident] + rotLN.war(rb) if cc == 0 else [tl]))
                rot_wnat.read(rb, tr_last)
                tev = p.op("dve", lambda E, rb=rb, pb=pb, R=R, wT=wT: E.tensor_copy(
                    bass.AP(wT, rb * 128, [[NCH * R, 128], [R, NCH], [1, 128]]), pb[:]),
                    after=[tr_last])
                rotLN.wrote(rb, tev)
                toks.append(tev)
            return toks

        def gen_proj_add(sub, s_i, qc_i, wT, biasv, outf_ap, out_dep, w_dep,
                         src, dst, PB):
            (pPRJ, rotPRJ, pTRb, rotTR, outsb, rot_outsb) = PB
            x_tok = {}
            for w in range(2):
                ev_toks = {}
                for j in range(4):
                    oc = w * 4 + j
                    mi = ((s_i * NQC + qc_i) * 2 + w) * 4 + j
                    mm_last = None
                    for cc in range(NCH):
                        mm_last = p.op("pe", lambda E, oc=oc, cc=cc: E.matmul(
                            pPRJ[:], wT[:, cc * C + oc * 128: cc * C + (oc + 1) * 128],
                            outf_ap[:, ts(cc, QC)], start=(cc == 0), stop=(cc == NCH - 1),
                            skip_group_check=True),
                            after=(rotPRJ.war(mi) + out_dep + w_dep if cc == 0 else []))
                    ev = p.op("act", lambda E, oc=oc, j=j: E.activation(
                        out=outsb[:, ts(j, QC)], in_=pPRJ[:], func=AF.Identity,
                        bias=biasv[:, oc:oc + 1]),
                        after=[mm_last] + rot_outsb.war(mi) + tok_bias)
                    rotPRJ.wrote(mi, ev)
                    rot_outsb.wrote(mi, ev)
                    ev_toks[j] = (ev, mi)
                    yield mm_last
                for tt in range(4):
                    ji = (s_i * NQC + qc_i) * 4 + tt
                    ii = ((s_i * NQC + qc_i) * 2 + w) * 4 + tt
                    gt = (s_i * T + qc_i * QC + tt * 128) // 128
                    r0 = gt * 128
                    bank = pTRb[tt % 2]
                    trs = []
                    for j in range(4):
                        ev, mi = ev_toks[j]
                        tr = p.op("pe", lambda E, j=j, tt=tt, bank=bank: E.transpose(
                            bank[:, ts(j, 128)],
                            outsb[:, j * QC + tt * 128: j * QC + (tt + 1) * 128],
                            ident_f[:]),
                            after=[ev, tok_identf] + (rotTR.war(ii) if j == 0 else []))
                        rot_outsb.read(mi, tr)
                        trs.append(tr)
                    if w == 0:
                        xi = 2000 * sub + ji
                        lx = p.dma("sp", lambda E, r0=r0, xi=xi: E.dma_start(
                            out=xq[:, ts(xi % 4, C)], in_=src[r0:r0 + 128, :]),
                            after=rot_xq.war(xi) + src_dep(sub, gt))
                        rot_xq.wrote(xi, lx)
                        x_tok[tt] = (lx, xi)
                    lx, xi = x_tok[tt]
                    xslot = (ji % 2) * 4 + w * 2
                    ad = p.op("dve", lambda E, ji=ji, xi=xi, w=w, bank=bank: E.tensor_add(
                        out=xnew[:, (ji % 2) * C + w * QC: (ji % 2) * C + (w + 1) * QC],
                        in0=bank[:], in1=xq[:, (xi % 4) * C + w * QC: (xi % 4) * C + (w + 1) * QC]),
                        after=trs + [lx] + rot_xnew.war(xslot) + rot_xnew.war(xslot + 1))
                    rot_xq.read(xi, ad)
                    rotTR.wrote(ii, ad)
                    sx = p.dma("sp", lambda E, r0=r0, ji=ji, w=w: E.dma_start(
                        out=dst[r0:r0 + 128, w * QC:(w + 1) * QC],
                        in_=xnew[:, (ji % 2) * C + w * QC: (ji % 2) * C + (w + 1) * QC]),
                        after=[ad])
                    rot_xnew.wrote(xslot, ad)
                    rot_xnew.wrote(xslot + 1, ad)
                    rot_xnew.read(xslot, sx)
                    rot_xnew.read(xslot + 1, sx)
                    xstore[(sub, gt, w)] = sx
                    yield None

        def attn_sublayer(sub, kind):
            src, dst = xsrc(sub), xdst(sub)
            subf = p.fence()
            wqkv = wqkv_a if kind == "a" else wqkv_b
            wproj = wproj_a if kind == "a" else wproj_b
            biasp = bias_pa if kind == "a" else bias_pb
            with ExitStack() as st:
                ee = st.enter_context
                wqkvT = ee(nc.sbuf_tensor(f"wqkvT{sub}", [128, NCH * 3 * C], BF16))
                yT = ee(nc.sbuf_tensor(f"yT{sub}", [128, 2 * NCH * QC], BF16))
                rot_yT = Rot(2)
                wprojT = ee(nc.sbuf_tensor(f"wprojT{sub}", [128, NCH * C], BF16))
                scr2 = ee(nc.sbuf_tensor(f"scr2_{sub}", [128, (2 if kind == "a" else 1) * QC], F32))
                outf = ee(nc.sbuf_tensor(
                    f"outf{sub}", [128, (NCH if kind == "a" else 2 * NCH) * QC], BF16))
                outsb = ee(nc.sbuf_tensor(f"outsb{sub}", [128, 4 * QC], F32))
                reps = ee(nc.sbuf_tensor(f"reps{sub}", [128, 2 * QC], F32))
                rrow = ee(nc.sbuf_tensor(f"rrow{sub}", [128, QC], F32R))
                rrowf = ee(nc.sbuf_tensor(f"rrowf{sub}", [128, QC], F32))
                if kind == "a":
                    scr1 = ee(nc.sbuf_tensor(f"scr1_{sub}", [128, 2 * QC], F32))
                    scr0 = ee(nc.sbuf_tensor(f"scr0_{sub}", [128, 2 * QC], F32))
                    rot_scr0 = Rot(2)
                    q_feat = ee(nc.sbuf_tensor(f"qf{sub}", [128, NCH * T], BF16))
                    kvacc = ee(nc.sbuf_tensor(f"kvacc{sub}", [128, 16 * 65], F32))
                    kvsb = ee(nc.sbuf_tensor(f"kvsb{sub}", [128, 8 * 65], BF16))
                    ktok = ee(nc.sbuf_tensor(f"ktok{sub}", [128, 2 * C], BF16))
                    vtok = ee(nc.sbuf_tensor(f"vtok{sub}", [128, 2 * H * 65], BF16))
                else:
                    vtok = ee(nc.sbuf_tensor(f"vtok{sub}", [128, NTT * H * 65], BF16))
                    expb = ee(nc.sbuf_tensor(f"expb{sub}", [128, 4 * QC], BF16))
                    avf = ee(nc.sbuf_tensor(f"avf{sub}", [128, 2 * QC], F32))
                    qstage = expb
                    kaug = ee(nc.sbuf_tensor(f"kaug{sub}", [128, 2 * T], BF16))
                    qsq = ee(nc.sbuf_tensor(f"qsq{sub}", [128, QC], BF16))
                    qaug = ee(nc.sbuf_tensor(f"qaug{sub}", [128, 2 * QC], BF16))
                    ksq = ee(nc.sbuf_tensor(f"ksq{sub}", [128, T], BF16))
                    ones_bf = ee(nc.sbuf_tensor(f"onesb{sub}", [128, 1], BF16))
                    nrm = ee(nc.sbuf_tensor(f"nrm{sub}", [128, 32], F32))

                pLNf = [ee(nc.psum_tensor(f"pLN{sub}_{i}", [128, QC], F32))
                        for i in range(2)]
                pLN = [t[:].bitcast(BF16) for t in pLNf]
                if kind == "a":
                    pMM = [ee(nc.psum_tensor(f"pMM{sub}_{i}", [128, QC], F32))
                           for i in range(2)]
                    pKV = [ee(nc.psum_tensor(f"pKV{sub}_{i}", [128, QC], F32))
                           for i in range(4)]
                    pREPs = [pKV[0], pKV[2]]
                    pPRJ = pKV[1]
                else:
                    pSC = [ee(nc.psum_tensor(f"pSC{sub}_{i}", [128, 2 * QC], F32))
                           for i in range(2)]
                    pMM = [pSC[0][:, 0:QC], pSC[1][:, 0:QC]]
                    pKV = [ee(nc.psum_tensor(f"pKV{sub}_{i}", [128, QC], F32))
                           for i in range(2)]
                    pPRJ = pKV[1]
                pTRb = [t[:] for t in pLNf]

                rotLN = Rot(2)
                rotMM = Rot(2)
                rotKV = Rot(4) if kind == "a" else Rot(2)
                rotREP = Rot(len(pREPs)) if kind == "a" else Rot(1)

                rotPRJ = Rot(1)
                rotTR = Rot(2)
                rot_outsb = Rot(4)
                rot_scr2 = Rot(2)
                rot_reps = Rot(2)
                rot_rrow = Rot(3)
                rot_outf = Rot(1) if kind == "a" else Rot(2)
                PB = (pPRJ, rotPRJ, pTRb, rotTR, outsb, rot_outsb)

                wnat = (q_feat if kind == "a" else ksq)
                rot_wnat = Rot(2)
                wq_toks = emit_wT(wqkv, 3 * C, wqkvT, wnat, rot_wnat, pLN, rotLN, 0,
                                  extra=subf + (tok_bias if sub == 0 else []))
                wp_toks = emit_wT(wproj, C, wprojT, wnat, rot_wnat, pLN, rotLN, 0)
                wq_done = wq_toks[-1]
                wp_done = wp_toks[-1]

                if kind == "a":
                    von = [p.op("dve", lambda E: E.memset(vtok[:], 1.0), after=subf)] * 2
                    rot_scr1 = Rot(2)
                    rot_ktok = Rot(2)
                    rot_vtok = Rot(2)
                    rot_kvacc = Rot(1)
                    rot_kvsb = Rot(1)
                else:
                    von_tok = p.op("dve", lambda E: E.memset(vtok[:], 1.0), after=subf)
                    kneg = p.op("dve", lambda E: E.memset(kaug[64:65, :], -1.0),
                                after=subf)
                    tko = p.op("dve", lambda E: E.memset(ones_bf[:], 1.0), after=subf)
                    rot_qstage = Rot(2)
                    rot_exp = Rot(2)
                    rot_vtok = Rot(NTT)
                    rot_kaug = Rot(2)
                    rot_qsq = Rot(1)
                    rotSC = Rot(2)
                    rot_avf = Rot(2)
                    rot_qaug = Rot(2)
                    rot_ksq = Rot(1)
                    rot_nrm = Rot(1)

                sample_fence = subf
                for s_i in range(S):
                    if kind == "a":
                        kvz = p.op("dve", lambda E: E.memset(kvacc[:], 0.0),
                                   after=rot_kvacc.war(s_i) + sample_fence)
                    for qc_i in range(NQC):
                        gi = s_i * NQC + qc_i
                        yT_tok, _, yT_sl = emit_ln_qchunk(
                            sub, src, s_i, qc_i, gi, pLN, rotLN, yT, 2, rot_yT,
                            extra_dep=(sample_fence if qc_i == 0 else ()))
                        if dbg and sub == 0 and gi == 0:
                            dtok = p.dma("sp", lambda E: E.dma_start(
                                out=dbg_yT, in_=yT[:, 0:NCH * QC]), after=[yT_tok])
                            rot_yT.read(gi, dtok)
                        ydep = [yT_tok, wq_done]

                        for ec in range(NCH):
                            mi = gi * 40 + ec
                            mm_last = None
                            for cc in range(NCH):
                                mm_last = p.op("pe", lambda E, ec=ec, cc=cc, mi=mi,
                                               yT_sl=yT_sl: E.matmul(
                                    pMM[mi % 2][:],
                                    wqkvT[:, cc * 3 * C + ec * 128: cc * 3 * C + (ec + 1) * 128],
                                    yT[:, yT_sl * NCH * QC + cc * QC: yT_sl * NCH * QC + (cc + 1) * QC],
                                    start=(cc == 0), stop=(cc == NCH - 1),
                                    skip_group_check=True),
                                    after=(rotMM.war(mi) + ydep if cc == 0 else []))
                            rot_yT.read(gi, mm_last)
                            qcol = ec * T + qc_i * QC
                            if kind == "a":
                                e0 = p.op("act", lambda E, mi=mi: E.activation(
                                    out=scr0[:, ts(mi % 2, QC)], in_=pMM[mi % 2][:],
                                    func=AF.Identity),
                                    after=[mm_last] + rot_scr0.war(mi))
                                rotMM.wrote(mi, e0)
                                e1 = p.op("act", lambda E, mi=mi: E.activation(
                                    out=scr1[:, ts(mi % 2, QC)], in_=scr0[:, ts(mi % 2, QC)],
                                    func=AF.Relu, scale=consts[:, 6:7]),
                                    after=[e0, tok_consts] + rot_scr1.war(mi))
                                e2 = p.op("act", lambda E, mi=mi: E.activation(
                                    out=scr2[:, ts(mi % 2, QC)], in_=scr1[:, ts(mi % 2, QC)],
                                    func=AF.Exp, scale=consts[:, 6:7]),
                                    after=[e1] + rot_scr2.war(mi))
                                e3 = p.op("dve", lambda E, mi=mi, qcol=qcol: E.scalar_tensor_tensor(
                                    out=q_feat[:, qcol:qcol + QC],
                                    in0=scr0[:, ts(mi % 2, QC)],
                                    scalar=0.0, in1=scr2[:, ts(mi % 2, QC)],
                                    op0=OP.max, op1=OP.add), after=[e2])
                                rot_scr0.wrote(mi, e3)
                                rot_scr1.wrote(mi, e1)
                                rot_scr2.wrote(mi, e3)
                            else:
                                e3 = p.op("act", lambda E, mi=mi: E.mul(
                                    out=qstage[:, ts(mi % 2, QC)], in_=pMM[mi % 2][:],
                                    mul=SCALE), after=[mm_last] + rot_qstage.war(mi))
                                rotMM.wrote(mi, e3)
                                sq = p.dma("sp", lambda E, mi=mi, ec=ec, qc_i=qc_i: E.dma_start(
                                    out=qspill[ec * 128:(ec + 1) * 128,
                                               qc_i * QC:(qc_i + 1) * QC],
                                    in_=qstage[:, ts(mi % 2, QC)]), after=[e3])
                                rot_qstage.wrote(mi, e3)
                                rot_qstage.read(mi, sq)
                                xstore[("q", sub, s_i, ec, qc_i)] = sq

                        if kind == "a":
                            for tt in range(4):
                                ti = gi * 4 + tt
                                tcol = yT_sl * NCH * QC + tt * 128
                                kd = None
                                for eb in range(2):
                                    mi = gi * 40 + 8 + tt * 4 + eb
                                    mm_last = None
                                    for cc in range(NCH):
                                        mm_last = p.op("pe", lambda E, cc=cc, eb=eb, mi=mi,
                                                       tcol=tcol: E.matmul(
                                            pMM[mi % 2][:],
                                            yT[:, cc * QC + tcol: cc * QC + tcol + 128],
                                            wqkvT[:, cc * 3 * C + C + eb * QC: cc * 3 * C + C + (eb + 1) * QC],
                                            start=(cc == 0), stop=(cc == NCH - 1),
                                            skip_group_check=True),
                                            after=(rotMM.war(mi) + [yT_tok, wq_done]
                                                   if cc == 0 else []))
                                    rot_yT.read(gi, mm_last)
                                    e0 = p.op("act", lambda E, mi=mi: E.activation(
                                        out=scr0[:, ts(mi % 2, QC)], in_=pMM[mi % 2][:],
                                        func=AF.Identity),
                                        after=[mm_last] + rot_scr0.war(mi))
                                    rotMM.wrote(mi, e0)
                                    e1 = p.op("act", lambda E, mi=mi: E.activation(
                                        out=scr1[:, ts(mi % 2, QC)], in_=scr0[:, ts(mi % 2, QC)],
                                        func=AF.Relu, scale=consts[:, 6:7]),
                                        after=[e0, tok_consts] + rot_scr1.war(mi))
                                    e2 = p.op("act", lambda E, mi=mi: E.activation(
                                        out=scr2[:, ts(mi % 2, QC)],
                                        in_=scr1[:, ts(mi % 2, QC)], func=AF.Exp,
                                        scale=consts[:, 6:7]),
                                        after=[e1] + rot_scr2.war(mi))
                                    e3 = p.op("dve", lambda E, mi=mi, ti=ti, eb=eb: E.scalar_tensor_tensor(
                                        out=ktok[:, (ti % 2) * C + eb * QC:(ti % 2) * C + (eb + 1) * QC],
                                        in0=scr0[:, ts(mi % 2, QC)], scalar=0.0,
                                        in1=scr2[:, ts(mi % 2, QC)], op0=OP.max, op1=OP.add),
                                        after=[e2] + (rot_ktok.war(ti) if eb == 0 else []))
                                    rot_scr0.wrote(mi, e3)
                                    rot_scr1.wrote(mi, e1)
                                    rot_scr2.wrote(mi, e3)
                                    kd = e3
                                rot_ktok.wrote(ti, kd)
                                vd = None
                                for eb in range(2):
                                    mi = gi * 40 + 8 + tt * 4 + 2 + eb
                                    mm_last = None
                                    for cc in range(NCH):
                                        mm_last = p.op("pe", lambda E, cc=cc, eb=eb, mi=mi,
                                                       tcol=tcol: E.matmul(
                                            pMM[mi % 2][:],
                                            yT[:, cc * QC + tcol: cc * QC + tcol + 128],
                                            wqkvT[:, cc * 3 * C + 2 * C + eb * QC: cc * 3 * C + 2 * C + (eb + 1) * QC],
                                            start=(cc == 0), stop=(cc == NCH - 1),
                                            skip_group_check=True),
                                            after=(rotMM.war(mi) + [yT_tok, wq_done]
                                                   if cc == 0 else []))
                                    rot_yT.read(gi, mm_last)
                                    ev = p.op("dve", lambda E, mi=mi, ti=ti, eb=eb: E.tensor_copy(
                                        bass.AP(vtok, (ti % 2) * H * 65 + eb * 8 * 65,
                                                [[2 * H * 65, 128], [65, 8], [1, 64]]),
                                        pMM[mi % 2][:]),
                                        after=[mm_last] + (rot_vtok.war(ti) if eb == 0 else [])
                                              + ([von[ti % 2]] if ti < 2 else []))
                                    rotMM.wrote(mi, ev)
                                    vd = ev
                                rot_vtok.wrote(ti, vd)
                                mmkv_last = {}
                                for h in range(H):
                                    slot = h % 4
                                    bk = pKV[h // 4]
                                    mmkv = p.op("pe", lambda E, h=h, bk=bk,
                                                slot=slot, ti=ti: E.matmul(
                                        bk[0:64, ts(slot, 65)],
                                        ktok[:, (ti % 2) * C + h * 64:(ti % 2) * C + (h + 1) * 64],
                                        vtok[:, (ti % 2) * H * 65 + h * 65:(ti % 2) * H * 65 + (h + 1) * 65],
                                        start=(slot == 0), stop=(slot == 3),
                                        skip_group_check=True),
                                        after=[kd, vd] + (rotKV.war(h // 4) if slot == 0 else []))
                                    rot_ktok.read(ti, mmkv)
                                    rot_vtok.read(ti, mmkv)
                                    mmkv_last[h // 4] = mmkv
                                for b in range(4):
                                    ka = p.op("dve", lambda E, b=b: E.tensor_add(
                                        out=kvacc[0:64, b * 4 * 65:(b + 1) * 4 * 65],
                                        in0=kvacc[0:64, b * 4 * 65:(b + 1) * 4 * 65],
                                        in1=pKV[b][0:64, 0:4 * 65]),
                                        after=[mmkv_last[b], kvz])
                                    rotKV.wrote(b, ka)
                        else:
                            for ec in range(NCH):
                                mi = gi * 40 + 8 + ec
                                mm_last = None
                                for cc in range(NCH):
                                    mm_last = p.op("pe", lambda E, ec=ec, cc=cc, mi=mi,
                                                   yT_sl=yT_sl: E.matmul(
                                        pMM[mi % 2][:],
                                        wqkvT[:, cc * 3 * C + C + ec * 128: cc * 3 * C + C + (ec + 1) * 128],
                                        yT[:, yT_sl * NCH * QC + cc * QC: yT_sl * NCH * QC + (cc + 1) * QC],
                                        start=(cc == 0), stop=(cc == NCH - 1),
                                        skip_group_check=True),
                                        after=(rotMM.war(mi) + ydep if cc == 0 else []))
                                rot_yT.read(gi, mm_last)
                                ekv = p.op("dve", lambda E, mi=mi: E.tensor_copy(
                                    qstage[:, ts(mi % 2, QC)], pMM[mi % 2][:]),
                                    after=[mm_last] + rot_qstage.war(mi))
                                rotMM.wrote(mi, ekv)
                                sk = p.dma("sp", lambda E, mi=mi, ec=ec, qc_i=qc_i: E.dma_start(
                                    out=kspill[ec * 128:(ec + 1) * 128, qc_i * QC:(qc_i + 1) * QC],
                                    in_=qstage[:, ts(mi % 2, QC)]), after=[ekv])
                                rot_qstage.wrote(mi, ekv)
                                rot_qstage.read(mi, sk)
                                xstore[("k", sub, s_i, ec, qc_i)] = sk
                            for tt in range(4):
                                ti = gi * 4 + tt
                                tcol = yT_sl * NCH * QC + tt * 128
                                glob_t = qc_i * 4 + tt
                                vti = s_i * NTT + glob_t
                                for eb in range(2):
                                    mi = gi * 40 + 20 + tt * 2 + eb
                                    mm_last = None
                                    for cc in range(NCH):
                                        mm_last = p.op("pe", lambda E, cc=cc, eb=eb, mi=mi,
                                                       tcol=tcol: E.matmul(
                                            pMM[mi % 2][:],
                                            yT[:, cc * QC + tcol: cc * QC + tcol + 128],
                                            wqkvT[:, cc * 3 * C + 2 * C + eb * QC: cc * 3 * C + 2 * C + (eb + 1) * QC],
                                            start=(cc == 0), stop=(cc == NCH - 1),
                                            skip_group_check=True),
                                            after=(rotMM.war(mi) + [yT_tok, wq_done]
                                                   if cc == 0 else []))
                                    rot_yT.read(gi, mm_last)
                                    ev = p.op("dve", lambda E, mi=mi, glob_t=glob_t, eb=eb: E.tensor_copy(
                                        bass.AP(vtok, glob_t * H * 65 + eb * 8 * 65,
                                                [[NTT * H * 65, 128], [65, 8], [1, 64]]),
                                        pMM[mi % 2][:]),
                                        after=[mm_last]
                                              + (rot_vtok.war(vti) if eb == 0 else [])
                                              + ([von_tok] if s_i == 0 and qc_i == 0 else []))
                                    rotMM.wrote(mi, ev)
                                    vd = ev
                                rot_vtok.wrote(vti, vd)

                    p1f = p.fence()
                    if kind == "a":
                        kv_last = [rotKV.wtok(b) for b in range(4)]
                        tkv = None
                        for h in range(H):
                            par, pp = h % 2, h // 2
                            tkv = p.op("dve", lambda E, h=h, par=par, pp=pp: E.tensor_copy(
                                kvsb[par * 64: par * 64 + 64, ts(pp, 65)],
                                kvacc[0:64, h * 65:(h + 1) * 65]),
                                after=kv_last + rot_kvsb.war(s_i))
                        rot_kvacc.wrote(s_i, tkv)
                        kv_dep = [tkv]
                        if dbg and sub == 0 and s_i == 0:
                            p.dma("sp", lambda E: E.dma_start(
                                out=dbg_kv, in_=kvacc[:]), after=[tkv])
                            dqt = p.dma("sp", lambda E: E.dma_start(
                                out=dbg_q, in_=q_feat[:]), after=p1f)
                        for qc_i in range(NQC):
                            out_toks = []
                            for h in range(H):
                                par = h % 2
                                pp = h // 2
                                mi = (s_i * NQC + qc_i) * H + h
                                qcol = pp * T + qc_i * QC
                                mo = p.op("pe", lambda E, par=par, pp=pp, mi=mi, qcol=qcol: E.matmul(
                                    pMM[mi % 2][0:65, :],
                                    kvsb[par * 64: par * 64 + 64, ts(pp, 65)],
                                    q_feat[par * 64: par * 64 + 64, qcol: qcol + QC],
                                    start=True, stop=True, skip_group_check=True),
                                    after=rotMM.war(mi) + kv_dep + p1f)
                                rot_kvsb.read(s_i, mo)
                                d1 = p.op("dve", lambda E, mi=mi: E.tensor_scalar_add(
                                    out=scr1[64:65, ts(mi % 2, QC)], in0=pMM[mi % 2][64:65, :],
                                    scalar1=AEPS), after=[mo] + rot_scr1.war(mi))
                                d2 = p.op("dve", lowp(nc, lambda E, mi=mi: E.reciprocal(
                                    out=rrow[64:65, :], in_=scr1[64:65, ts(mi % 2, QC)])),
                                    after=[d1] + rot_rrow.war(0))
                                rot_scr1.wrote(mi, d2)
                                pR = pREPs[h % len(pREPs)]
                                rs = h % 2
                                d3 = p.op("pe", lowp(nc, lambda E, pR=pR: E.matmul(
                                    pR[0:64, :], ones_r[64:65, :], rrow[64:65, :],
                                    start=True, stop=True, skip_group_check=True)),
                                    after=[d2, tok_ones] + rotREP.war(h))
                                rot_rrow.wrote(0, d3)
                                d4 = p.op("act", lambda E, pR=pR, rs=rs: E.copy(
                                    out=reps[0:64, ts(rs, QC)], in_=pR[0:64, :]),
                                    after=[d3] + rot_reps.war(rs))
                                rotREP.wrote(h, d4)
                                rot_reps.wrote(rs, d4)
                                ocol = pp * QC
                                if par == 0:
                                    d5 = p.op("dve", lambda E, mi=mi, ocol=ocol, rs=rs: E.tensor_tensor(
                                        out=outf[0:64, ocol: ocol + QC],
                                        in0=pMM[mi % 2][0:64, :], in1=reps[0:64, ts(rs, QC)],
                                        op=OP.mult), after=[d4] + rot_outf.war(0))
                                    rotMM.wrote(mi, d5)
                                    rot_reps.read(rs, d5)
                                    out_toks.append(d5)
                                else:
                                    d5 = p.op("dve", lambda E, mi=mi, rs=rs: E.tensor_tensor(
                                        out=scr2[0:64, ts(mi % 2, QC)],
                                        in0=pMM[mi % 2][0:64, :], in1=reps[0:64, ts(rs, QC)],
                                        op=OP.mult), after=[d4] + rot_scr2.war(mi))
                                    d6 = p.op("dve", lambda E, mi=mi, ocol=ocol: E.tensor_copy(
                                        outf[64:128, ocol: ocol + QC],
                                        scr2[0:64, ts(mi % 2, QC)]),
                                        after=[d5] + rot_outf.war(0))
                                    rotMM.wrote(mi, d5)
                                    rot_scr2.wrote(mi, d6)
                                    rot_reps.read(rs, d5)
                                    out_toks.append(d6)
                            if dbg and sub == 0 and s_i == 0 and qc_i == 0:
                                dot = p.dma("sp", lambda E: E.dma_start(
                                    out=dbg_outf, in_=outf[:]), after=out_toks)
                                rot_outf.read(0, dot)
                            rot_outf.wrote(0, out_toks[-1])
                            for _tok in gen_proj_add(sub, s_i, qc_i, wprojT, biasp,
                                                     outf, out_toks, [wp_done],
                                                     src, dst, PB):
                                if _tok is not None:
                                    rot_outf.read(0, _tok)
                    else:
                        maxk_tok = {}
                        pend = None
                        for qc_i in range(NQC):
                            os_ = qc_i % 2
                            outf_ap = outf[:, os_ * NCH * QC:(os_ + 1) * NCH * QC]
                            out_toks = []
                            for h in range(H):
                                par = h % 2
                                pp = h // 2
                                hh = (s_i * NQC + qc_i) * H + h
                                lk = p.dma("sp", lambda E, h=h, hh=hh: E.dma_start(
                                    out=kaug[0:64, ts(hh % 2, T)],
                                    in_=kspill[h * 64:(h + 1) * 64, :]),
                                    after=rot_kaug.war(hh) + p1f
                                          + [xstore[("k", sub, s_i, pp, qq)] for qq in range(NQC)])
                                lq = p.dma("sp", lambda E, h=h, hh=hh, qc_i=qc_i: E.dma_start(
                                    out=qaug[0:64, ts(hh % 2, QC)],
                                    in_=qspill[h * 64:(h + 1) * 64, qc_i * QC:(qc_i + 1) * QC]),
                                    after=rot_qaug.war(hh) + p1f
                                          + [xstore[("q", sub, s_i, pp, qc_i)]])
                                if qc_i == 0:
                                    sqk = p.op("act", lambda E, hh=hh: E.activation(
                                        out=ksq[0:64, :], in_=kaug[0:64, ts(hh % 2, T)],
                                        func=AF.Square),
                                        after=[lk] + rot_ksq.war(0))
                                    rds = []
                                    for j in range(4):
                                        mmn = p.op("pe", lambda E, j=j: E.matmul(
                                            pPRJ[0:1, :], ones_bf[0:64, 0:1],
                                            ksq[0:64, j * QC:(j + 1) * QC],
                                            start=True, stop=True, skip_group_check=True),
                                            after=[sqk, tko] + rotPRJ.war(0))
                                        rd = p.op("dve", lambda E, j=j: E.reduce_max(
                                            out=nrm[0:1, j:j + 1], in_=pPRJ[0:1, :],
                                            axis=mybir.AxisListType.X),
                                            after=[mmn] + rot_nrm.war(0))
                                        rotPRJ.wrote(0, rd)
                                        rds.append(rd)
                                        rot_ksq.read(0, mmn)
                                    rd2 = p.op("dve", lambda E: E.reduce_max(
                                        out=nrm[0:1, 8:9], in_=nrm[0:1, 0:4],
                                        axis=mybir.AxisListType.X), after=rds)
                                    mkl = p.op("act", lambda E: E.activation(
                                        out=nrm[0:1, 9:10], in_=nrm[0:1, 8:9],
                                        func=AF.Ln), after=[rd2])
                                    mk = p.op("act", lambda E, h=h: E.activation(
                                        out=nrm[0:1, 16 + h:17 + h], in_=nrm[0:1, 9:10],
                                        func=AF.Exp, scale=consts[0:1, 4:5]),
                                        after=[mkl, tok_consts])
                                    rot_nrm.wrote(0, mk)
                                    maxk_tok[h] = mk
                                q2 = p.op("act", lambda E, hh=hh: E.activation(
                                    out=qsq[0:64, 0:QC], in_=qaug[0:64, ts(hh % 2, QC)],
                                    func=AF.Square), after=[lq] + rot_qsq.war(0))
                                mmq = p.op("pe", lambda E: E.matmul(
                                    pPRJ[0:1, :], ones_bf[0:64, 0:1], qsq[0:64, 0:QC],
                                    start=True, stop=True, skip_group_check=True),
                                    after=[q2, tko] + rotPRJ.war(0))
                                rot_qsq.wrote(0, q2)
                                rot_qsq.read(0, mmq)
                                qnl = p.op("act", lambda E: E.activation(
                                    out=rrowf[0:1, :],
                                    in_=pPRJ[0:1, :], func=AF.Ln),
                                    after=[mmq] + rot_rrow.war(2))
                                qn = p.op("act", lambda E: E.activation(
                                    out=scr2[0:1, 0:QC], in_=rrowf[0:1, :], func=AF.Exp,
                                    scale=consts[0:1, 4:5]),
                                    after=[qnl, tok_consts] + rot_scr2.war(0))
                                rot_rrow.wrote(2, qn)
                                rotPRJ.wrote(0, qnl)
                                cr1 = p.op("dve", lowp(nc, lambda E, h=h: E.tensor_scalar_mul(
                                    out=rrow[0:1, :], in0=scr2[0:1, 0:QC],
                                    scalar1=nrm[0:1, 16 + h:17 + h])),
                                    after=[qn, maxk_tok[h]] + rot_rrow.war(1))
                                rot_scr2.wrote(0, cr1)
                                rot_nrm.read(0, cr1)
                                cr2 = p.op("dve", lowp(nc, lambda E, hh=hh: E.tensor_copy(
                                    qaug[64:65, ts(hh % 2, QC)], rrow[0:1, :])),
                                    after=[cr1])
                                rot_rrow.wrote(1, cr2)
                                def emit_msc(kc):
                                    pr = (kc // 2) % 2
                                    t = p.op("pe", lambda E, kc=kc, hh=hh, pr=pr: E.matmul(
                                        pSC[pr][:, (kc % 2) * QC:(kc % 2 + 1) * QC],
                                        kaug[0:65, (hh % 2) * T + kc * 128:(hh % 2) * T + (kc + 1) * 128],
                                        qaug[0:65, ts(hh % 2, QC)],
                                        start=True, stop=True, skip_group_check=True),
                                        after=(rotSC.war(kc // 2) if kc % 2 == 0 else [])
                                              + [cr2] + ([lk, kneg] if kc == 0 else []))
                                    return t

                                def emit_exp(kc2, mm1, mm2):
                                    pr = kc2 % 2
                                    ex = p.op("act", lambda E, pr=pr: E.activation(
                                        out=expb[:, ts(pr, 2 * QC)], in_=pSC[pr][:],
                                        func=AF.Exp),
                                        after=[mm1, mm2] + rot_exp.war(kc2))
                                    rotSC.wrote(kc2, ex)
                                    if dbg and s_i == 0 and qc_i == 0 and h == 0 and kc2 == 0:
                                        dex = p.dma("sp", lambda E: E.dma_start(
                                            out=dbg_expb, in_=expb[:, 0:QC]), after=[ex])
                                        rot_exp.read(kc2, dex)
                                    return ex

                                def emit_mav(kc, ex):
                                    pr = (kc // 2) % 2
                                    mav = p.op("pe", lambda E, h=h, kc=kc, pr=pr: E.matmul(
                                        pKV[0][0:65, :],
                                        vtok[:, kc * H * 65 + h * 65: kc * H * 65 + (h + 1) * 65],
                                        expb[:, pr * 2 * QC + (kc % 2) * QC: pr * 2 * QC + (kc % 2 + 1) * QC],
                                        start=(kc == 0), stop=(kc == NTT - 1),
                                        skip_group_check=True),
                                        after=[ex] + (rotKV.war(0) if kc == 0 else []))
                                    if kc % 2 == 1:
                                        rot_exp.wrote(kc // 2, mav)
                                    rot_vtok.read(s_i * NTT + kc, mav)
                                    return mav

                                mm1 = emit_msc(0)
                                mm2 = emit_msc(1)
                                ex_p = emit_exp(0, mm1, mm2)
                                av_last = None
                                for kc2 in range(1, NTT // 2):
                                    mm1 = emit_msc(2 * kc2)
                                    mm2 = emit_msc(2 * kc2 + 1)
                                    emit_mav(2 * kc2 - 2, ex_p)
                                    emit_mav(2 * kc2 - 1, ex_p)
                                    ex_p = emit_exp(kc2, mm1, mm2)
                                emit_mav(NTT - 2, ex_p)
                                av_last = emit_mav(NTT - 1, ex_p)
                                msc_p = mm2
                                rot_kaug.read(hh, msc_p)
                                rot_qaug.read(hh, msc_p)
                                rs = h % 2
                                e0 = p.op("dve", lambda E, rs=rs: E.tensor_copy(
                                    avf[0:65, ts(rs, QC)], pKV[0][0:65, :]),
                                    after=[av_last] + rot_avf.war(rs))
                                rotKV.wrote(0, e0)
                                rot_avf.wrote(rs, e0)
                                d2 = p.op("dve", lowp(nc, lambda E, rs=rs: E.reciprocal(
                                    out=rrow[64:65, :], in_=avf[64:65, rs * QC:(rs + 1) * QC])),
                                    after=[e0] + rot_rrow.war(0))
                                rot_avf.read(rs, d2)
                                d3 = p.op("pe", lowp(nc, lambda E: E.matmul(
                                    pSC[1][0:64, 0:QC], ones_r[64:65, :], rrow[64:65, :],
                                    start=True, stop=True, skip_group_check=True)),
                                    after=[d2, tok_ones] + rotSC.war(1))
                                rot_rrow.wrote(0, d3)
                                d4 = p.op("act", lambda E, rs=rs: E.copy(
                                    out=reps[0:64, ts(rs, QC)], in_=pSC[1][0:64, 0:QC]),
                                    after=[d3] + rot_reps.war(rs))
                                rotSC.wrote(1, d4)
                                rot_reps.wrote(rs, d4)
                                ocol = pp * QC
                                if par == 0:
                                    d5 = p.op("dve", lambda E, ocol=ocol, rs=rs, os_=os_: E.tensor_tensor(
                                        out=outf[0:64, os_ * NCH * QC + ocol: os_ * NCH * QC + ocol + QC],
                                        in0=avf[0:64, rs * QC:(rs + 1) * QC],
                                        in1=reps[0:64, ts(rs, QC)],
                                        op=OP.mult), after=[d4] + rot_outf.war(os_))
                                    rot_avf.read(rs, d5)
                                    rot_reps.read(rs, d5)
                                    out_toks.append(d5)
                                else:
                                    d5 = p.op("dve", lambda E, rs=rs: E.tensor_tensor(
                                        out=scr2[0:64, 0:QC],
                                        in0=avf[0:64, rs * QC:(rs + 1) * QC],
                                        in1=reps[0:64, ts(rs, QC)],
                                        op=OP.mult), after=[d4] + rot_scr2.war(0))
                                    d6 = p.op("dve", lambda E, ocol=ocol, os_=os_: E.tensor_copy(
                                        outf[64:128, os_ * NCH * QC + ocol: os_ * NCH * QC + ocol + QC],
                                        scr2[0:64, 0:QC]),
                                        after=[d5] + rot_outf.war(os_))
                                    rot_avf.read(rs, d5)
                                    rot_scr2.wrote(0, d6)
                                    rot_reps.read(rs, d5)
                                    out_toks.append(d6)
                                if pend is not None:
                                    try:
                                        _tok = next(pend)
                                        if _tok is not None:
                                            rot_outf.read(1 - os_, _tok)
                                    except StopIteration:
                                        pend = None
                            if dbg and s_i == 0 and qc_i == 0:
                                dob = p.dma("sp", lambda E: E.dma_start(
                                    out=dbg_outfb, in_=outf[:]), after=out_toks)
                                rot_outf.read(0, dob)
                            if pend is not None:
                                for _tok in pend:
                                    if _tok is not None:
                                        rot_outf.read(1 - os_, _tok)
                                pend = None
                            rot_outf.wrote(os_, out_toks[-1])
                            pend = gen_proj_add(sub, s_i, qc_i, wprojT, biasp,
                                                outf_ap, out_toks, [wp_done],
                                                src, dst, PB)
                            if qc_i == NQC - 1:
                                for _tok in pend:
                                    if _tok is not None:
                                        rot_outf.read(os_, _tok)
                                pend = None
                    sample_fence = p.fence()

        def ffn_sublayer(sub):
            src, dst = xsrc(sub), xdst(sub)
            subf = p.fence()
            with ExitStack() as st:
                ee = st.enter_context
                w1sb = ee(nc.sbuf_tensor(f"w1sb{sub}", [128, NCH * FF], BF16))
                w2sb = ee(nc.sbuf_tensor(f"w2sb{sub}", [128, NFH * C], BF16))
                hbuf = ee(nc.sbuf_tensor(f"hbuf{sub}", [128, NFH * QC], BF16))
                yT = ee(nc.sbuf_tensor(f"yTF{sub}", [128, NCH * QC], BF16))
                rot_yT = Rot(1)
                outsb2 = ee(nc.sbuf_tensor(f"outsbF{sub}", [128, 2 * QC], F32))
                pLNf = [ee(nc.psum_tensor(f"pLNF{sub}_{i}", [128, QC], F32))
                        for i in range(2)]
                pLN = [t[:].bitcast(BF16) for t in pLNf]
                pTRb = [t[:] for t in pLNf]
                pH = [ee(nc.psum_tensor(f"pHF{sub}_{i}", [128, QC], F32)) for i in range(2)]
                pO = [ee(nc.psum_tensor(f"pOF{sub}_{i}", [128, QC], F32)) for i in range(2)]
                rotLN = Rot(2)
                rotH = Rot(2)
                rotO = Rot(2)
                rot_h = Rot(1)
                rot_outsb = Rot(2)

                w1_toks = [p.dma("pool", lambda E, cc=cc: E.dma_start(
                    out=w1sb[:, ts(cc, FF)], in_=w1d[cc * 128:(cc + 1) * 128, :]),
                    after=(subf if cc == 0 else []))
                    for cc in range(NCH)]
                w2_toks = [p.dma("pool", lambda E, hh=hh: E.dma_start(
                    out=w2sb[:, ts(hh, C)], in_=w2d[hh * 128:(hh + 1) * 128, :]))
                    for hh in range(NFH)]
                w1_done = w1_toks[-1]
                w2_done = w2_toks[-1]

                for s_i in range(S):
                    for qc_i in range(NQC):
                        gi = s_i * NQC + qc_i
                        yT_tok, x_toks, _ = emit_ln_qchunk(
                            sub, src, s_i, qc_i, gi, pLN, rotLN, yT, 1, rot_yT,
                            extra_dep=(subf if gi == 0 else ()))
                        if dbg and sub == 1 and gi == 0:
                            dt2 = p.dma("sp", lambda E: E.dma_start(
                                out=dbg_yT2, in_=yT[:]), after=[yT_tok])
                            rot_yT.read(gi, dt2)
                        h_last = None
                        first_fc_deps = rot_h.war(gi)
                        for fc in range(NFH):
                            mi = gi * 80 + fc
                            mm_last = None
                            for cc in range(NCH):
                                mm_last = p.op("pe", lambda E, fc=fc, cc=cc, mi=mi: E.matmul(
                                    pH[mi % 2][:],
                                    w1sb[:, cc * FF + fc * 128: cc * FF + (fc + 1) * 128],
                                    yT[:, ts(cc, QC)],
                                    start=(cc == 0), stop=(cc == NCH - 1),
                                    skip_group_check=True),
                                    after=(rotH.war(mi) + [yT_tok, w1_done]
                                           + (first_fc_deps if fc == 0 else [])
                                           if cc == 0 else []))
                            rot_yT.read(gi, mm_last)
                            gv = p.op("act", lambda E, fc=fc, mi=mi: E.activation(
                                out=hbuf[:, ts(fc, QC)], in_=pH[mi % 2][:], func=AF.Gelu,
                                bias=bias_1[:, fc:fc + 1]),
                                after=[mm_last] + tok_bias)
                            rotH.wrote(mi, gv)
                            h_last = gv
                        rot_h.wrote(gi, h_last)
                        if dbg and sub == 1 and gi == 0:
                            dh = p.dma("sp", lambda E: E.dma_start(
                                out=dbg_h, in_=hbuf[:]), after=[h_last])
                            rot_h.read(gi, dh)
                        ad_toks = {}
                        for w in range(4):
                            ev_toks = {}
                            for j in range(2):
                                oc = w * 2 + j
                                mi = (gi * 4 + w) * 2 + j
                                mm_last = None
                                for hh2 in range(NFH):
                                    mm_last = p.op("pe", lambda E, oc=oc, hh2=hh2, mi=mi: E.matmul(
                                        pO[mi % 2][:],
                                        w2sb[:, hh2 * C + oc * 128: hh2 * C + (oc + 1) * 128],
                                        hbuf[:, ts(hh2, QC)],
                                        start=(hh2 == 0), stop=(hh2 == NFH - 1),
                                        skip_group_check=True),
                                        after=(rotO.war(mi) + [h_last, w2_done]
                                               if hh2 == 0 else []))
                                rot_h.read(gi, mm_last)
                                ev = p.op("act", lambda E, oc=oc, j=j, mi=mi: E.activation(
                                    out=outsb2[:, ts(j, QC)], in_=pO[mi % 2][:],
                                    func=AF.Identity, bias=bias_2[:, oc:oc + 1]),
                                    after=[mm_last] + rot_outsb.war(mi) + tok_bias)
                                rotO.wrote(mi, ev)
                                rot_outsb.wrote(mi, ev)
                                if dbg and sub == 1 and gi == 0 and w == 0 and j == 1:
                                    dosb = p.dma("sp", lambda E: E.dma_start(
                                        out=dbg_osb, in_=outsb2[:]), after=[ev_toks[0][0], ev])
                                    rot_outsb.read(mi, dosb)
                                    rot_outsb.read(mi - 1, dosb)
                                ev_toks[j] = (ev, mi)
                            for tt in range(4):
                                ji = gi * 4 + tt
                                lnidx = gi * 16 + w * 4 + tt
                                bank = pTRb[tt % 2]
                                trs = []
                                for j in range(2):
                                    ev, mi = ev_toks[j]
                                    tr = p.op("pe", lambda E, j=j, tt=tt, bank=bank: E.transpose(
                                        bank[:, ts(j, 128)],
                                        outsb2[:, j * QC + tt * 128: j * QC + (tt + 1) * 128],
                                        ident_f[:]),
                                        after=[ev, tok_identf]
                                              + (rotLN.war(lnidx) if j == 0 else []))
                                    rot_outsb.read(mi, tr)
                                    trs.append(tr)
                                xi = gi * 4 + tt
                                xslot = (ji % 2) * 4 + w
                                gt = (s_i * T + qc_i * QC + tt * 128) // 128
                                r0 = gt * 128
                                ad = p.op("dve", lambda E, ji=ji, xi=xi, w=w, bank=bank: E.tensor_add(
                                    out=xnew[:, (ji % 2) * C + w * 256: (ji % 2) * C + (w + 1) * 256],
                                    in0=bank[:, 0:256],
                                    in1=xq[:, (xi % 4) * C + w * 256: (xi % 4) * C + (w + 1) * 256]),
                                    after=trs + [x_toks[tt]] + rot_xnew.war(xslot))
                                rot_xq.read(xi, ad)
                                rotLN.wrote(lnidx, ad)
                                rot_xnew.wrote(xslot, ad)
                                sx = p.dma("sp", lambda E, r0=r0, ji=ji, w=w: E.dma_start(
                                    out=dst[r0:r0 + 128, w * 256:(w + 1) * 256],
                                    in_=xnew[:, (ji % 2) * C + w * 256: (ji % 2) * C + (w + 1) * 256]),
                                    after=[ad])
                                rot_xnew.read(xslot, sx)
                                xstore[(sub, gt, "f%d" % w)] = sx

        seq = ["a", "f", "b", "f"]
        for sub in range(n_sub):
            if seq[sub] == "f":
                ffn_sublayer(sub)
            else:
                attn_sublayer(sub, seq[sub])

        with nc.Block() as block:
            p.emit(nc, block, stack)

    return nc


def kernel(L0_attn_a_x, L0_attn_a_Wqkv, L0_attn_a_Wproj, L0_attn_a_bproj,
           L0_attn_b_w_qkv, L0_attn_b_w_proj, L0_attn_b_b_proj,
           L0_ffn_w1, L0_ffn_b1, L0_ffn_w2, L0_ffn_b2, n_sub=4):
    x = np.ascontiguousarray(np.asarray(L0_attn_a_x, dtype=np.float32))
    B, T_, C_ = x.shape
    n = 8
    nc = build_program(n_sub=n_sub)
    ident = np.eye(128, dtype=ml_dtypes.bfloat16)
    base = {
        "wqkv_a": np.ascontiguousarray(np.asarray(L0_attn_a_Wqkv, dtype=np.float32)),
        "wproj_a": np.ascontiguousarray(np.asarray(L0_attn_a_Wproj, dtype=np.float32)),
        "bproj_a": np.ascontiguousarray(np.asarray(L0_attn_a_bproj, dtype=np.float32)),
        "wqkv_b": np.ascontiguousarray(np.asarray(L0_attn_b_w_qkv, dtype=np.float32)),
        "wproj_b": np.ascontiguousarray(np.asarray(L0_attn_b_w_proj, dtype=np.float32)),
        "bproj_b": np.ascontiguousarray(np.asarray(L0_attn_b_b_proj, dtype=np.float32)),
        "w1": np.ascontiguousarray(np.asarray(L0_ffn_w1, dtype=np.float32)),
        "b1": np.ascontiguousarray(np.asarray(L0_ffn_b1, dtype=np.float32)),
        "w2": np.ascontiguousarray(np.asarray(L0_ffn_w2, dtype=np.float32)),
        "b2": np.ascontiguousarray(np.asarray(L0_ffn_b2, dtype=np.float32)),
        "ident": ident,
    }
    in_maps = []
    for c in range(n):
        m = dict(base)
        m["xin"] = np.ascontiguousarray(x[c * S:(c + 1) * S].reshape(S * T_, C_))
        in_maps.append(m)
    res = run_bass_kernel_spmd(nc, in_maps, core_ids=list(range(n)))
    out = np.empty((B, T_, C_), dtype=np.float32)
    for c in range(n):
        out[c * S:(c + 1) * S] = np.asarray(res.results[c]["xout"]).reshape(S, T_, C_)
    return out


# version 06
# speedup vs baseline: 1.0065x; 1.0065x over previous
import numpy as np
import ml_dtypes
import concourse.bass as bass
import concourse.mybir as mybir
from concourse.bass_utils import run_bass_kernel_spmd
from contextlib import ExitStack

F32 = mybir.dt.float32
F32R = mybir.dt.float32r
BF16 = mybir.dt.bfloat16
AF = mybir.ActivationFunctionType
OP = mybir.AluOpType

S = 2
T = 2048
C = 1024
H = 16
HD = 64
FF = 4096
NCH = C // 128
NTT = T // 128
QC = 512
NQC = T // QC
NFH = FF // 128
LNEPS = 1e-5
AEPS = 1e-6
SCALE = 0.125
ESHIFT = -12.0

ENGINES = ("sp", "act", "dve", "pe", "pool")


class Plan:
    def __init__(self):
        self.ops = {e: [] for e in ENGINES}
        self.cnt = {e: 0 for e in ENGINES}

    def op(self, eng, fn, after=(), inc=1):
        waits = {}
        for tok in after:
            if tok is None:
                continue
            e2, v2 = tok
            if v2 > waits.get(e2, 0):
                waits[e2] = v2
        self.cnt[eng] += inc
        self.ops[eng].append((waits, fn, inc))
        return (eng, self.cnt[eng])

    def dma(self, eng, fn, after=()):
        return self.op(eng, fn, after, inc=16)

    def fence(self):
        return [(e, self.cnt[e]) for e in ENGINES if self.cnt[e] > 0]

    def emit(self, nc, block, stack):
        sems = {e: stack.enter_context(nc.semaphore(f"sem_{e}")) for e in ENGINES}
        engmap = {"sp": block.sync, "act": block.scalar, "dve": block.vector,
                  "pe": block.tensor, "pool": block.gpsimd}
        plan = self

        def mk(eng_name):
            def body(E):
                seen = {e: 0 for e in ENGINES}
                for waits, fn, inc in plan.ops[eng_name]:
                    for e2 in sorted(waits):
                        v2 = waits[e2]
                        if v2 > seen[e2]:
                            E.wait_ge(sems[e2], v2)
                            seen[e2] = v2
                    fn(E).then_inc(sems[eng_name], inc)
                if eng_name == "sp":
                    E.wait_ge(sems["sp"], plan.cnt["sp"])
                    if plan.cnt["pool"]:
                        E.wait_ge(sems["pool"], plan.cnt["pool"])
            return body

        for e in ENGINES:
            if plan.ops[e] or e == "sp":
                engmap[e](mk(e))


class Rot:

    def __init__(self, n):
        self.n = n
        self.wr = [None] * n
        self.rd = [[] for _ in range(n)]

    def war(self, i):
        s = i % self.n
        dep = list(self.rd[s])
        if self.wr[s] is not None:
            dep.append(self.wr[s])
        return dep

    def wrote(self, i, tok):
        s = i % self.n
        self.wr[s] = tok
        self.rd[s] = []

    def read(self, i, tok):
        self.rd[i % self.n].append(tok)

    def wtok(self, i):
        return self.wr[i % self.n]


def ts(i, sz):
    return slice(i * sz, (i + 1) * sz)


def lowp(nc, f):
    def g(E):
        with nc.allow_low_precision(reason="fp32r broadcast path"):
            return f(E)
    return g


def build_program(n_sub=4, dbg=False):
    nc = bass.Bass("TRN2", target_bir_lowering=False, debug=False)
    NT = S * T

    xin = nc.dram_tensor("xin", [NT, C], F32, kind="ExternalInput").ap()
    xout = nc.dram_tensor("xout", [NT, C], F32, kind="ExternalOutput").ap()
    wqkv_a = nc.dram_tensor("wqkv_a", [3 * C, C], F32, kind="ExternalInput").ap()
    wproj_a = nc.dram_tensor("wproj_a", [C, C], F32, kind="ExternalInput").ap()
    bproj_a = nc.dram_tensor("bproj_a", [C], F32, kind="ExternalInput").ap()
    wqkv_b = nc.dram_tensor("wqkv_b", [3 * C, C], F32, kind="ExternalInput").ap()
    wproj_b = nc.dram_tensor("wproj_b", [C, C], F32, kind="ExternalInput").ap()
    bproj_b = nc.dram_tensor("bproj_b", [C], F32, kind="ExternalInput").ap()
    w1d = nc.dram_tensor("w1", [C, FF], F32, kind="ExternalInput").ap()
    b1d = nc.dram_tensor("b1", [FF], F32, kind="ExternalInput").ap()
    w2d = nc.dram_tensor("w2", [FF, C], F32, kind="ExternalInput").ap()
    b2d = nc.dram_tensor("b2", [C], F32, kind="ExternalInput").ap()
    ident_in = nc.dram_tensor("ident", [128, 128], BF16, kind="ExternalInput").ap()

    xs = [nc.dram_tensor(f"xs{i}", [NT, C], F32, kind="Internal").ap() for i in range(2)]
    if dbg:
        dbg_yT = nc.dram_tensor("dbg_yT", [128, NCH * QC], BF16, kind="ExternalOutput").ap()
        dbg_q = nc.dram_tensor("dbg_q", [128, NCH * T], BF16, kind="ExternalOutput").ap()
        dbg_kv = nc.dram_tensor("dbg_kv", [128, 16 * 65], F32, kind="ExternalOutput").ap()
        dbg_outf = nc.dram_tensor("dbg_outf", [128, NCH * QC], BF16, kind="ExternalOutput").ap()
        dbg_prj = nc.dram_tensor("dbg_prj", [128, 4 * QC], F32, kind="ExternalOutput").ap()
        dbg_xnew = nc.dram_tensor("dbg_xnew", [128, C], F32, kind="ExternalOutput").ap()
        dbg_ptr = nc.dram_tensor("dbg_ptr", [128, QC], F32, kind="ExternalOutput").ap()
        dbg_yT2 = nc.dram_tensor("dbg_yT2", [128, NCH * QC], BF16, kind="ExternalOutput").ap()
        dbg_h = nc.dram_tensor("dbg_h", [128, NFH * QC], BF16, kind="ExternalOutput").ap()
        dbg_osb = nc.dram_tensor("dbg_osb", [128, 2 * QC], F32, kind="ExternalOutput").ap()
        dbg_outfb = nc.dram_tensor("dbg_outfb", [128, NCH * QC], BF16, kind="ExternalOutput").ap()
        dbg_expb = nc.dram_tensor("dbg_expb", [128, QC], BF16, kind="ExternalOutput").ap()
    qspill = nc.dram_tensor("qspill", [C, T], BF16, kind="Internal").ap()
    kspill = nc.dram_tensor("kspill", [C, T], BF16, kind="Internal").ap()

    def xsrc(sub):
        return xin if sub == 0 else xs[(sub - 1) % 2]

    def xdst(sub):
        return xout if sub == n_sub - 1 else xs[sub % 2]

    p = Plan()
    xstore = {}

    def src_dep(sub, gt):
        if sub == 0:
            return []
        return [xstore.get((sub - 1, gt, 0)), xstore.get((sub - 1, gt, 1)),
                xstore.get((sub - 1, gt, "f0")), xstore.get((sub - 1, gt, "f1")),
                xstore.get((sub - 1, gt, "f2")), xstore.get((sub - 1, gt, "f3")),
                xstore.get((sub - 1, gt))]

    with ExitStack() as stack:
        e = stack.enter_context

        ident = e(nc.sbuf_tensor("identb", [128, 128], BF16))
        ident_f = e(nc.sbuf_tensor("identf", [128, 128], F32))
        ones_r = e(nc.sbuf_tensor("ones_r", [128, 64], F32R))
        consts = e(nc.sbuf_tensor("consts", [128, 8], F32))
        bias_pa = e(nc.sbuf_tensor("bias_pa", [128, NCH], F32))
        bias_pb = e(nc.sbuf_tensor("bias_pb", [128, NCH], F32))
        bias_1 = e(nc.sbuf_tensor("bias_1", [128, NFH], F32))
        bias_2 = e(nc.sbuf_tensor("bias_2", [128, NCH], F32))
        stats = e(nc.sbuf_tensor("stats", [128, 2 * 12], F32))
        aggr = e(nc.sbuf_tensor("aggr", [128, 2 * 4], F32))
        lnsc = e(nc.sbuf_tensor("lnsc", [128, 2 * 32], F32))
        xq = e(nc.sbuf_tensor("xq", [128, 4 * C], F32))
        ybuf = e(nc.sbuf_tensor("ybuf", [128, 2 * C], BF16))
        xnew = e(nc.sbuf_tensor("xnew", [128, 2 * C], F32))

        tok_ident = p.dma("sp", lambda E: E.dma_start(out=ident[:], in_=ident_in))
        tok_identf = p.op("dve", lambda E: E.tensor_copy(ident_f[:], ident[:]),
                          after=[tok_ident])
        p.op("dve", lambda E: E.memset(consts[:, 0:1], LNEPS))
        p.op("dve", lambda E: E.memset(consts[:, 1:2], AEPS))
        p.op("dve", lambda E: E.memset(consts[:, 2:3], ESHIFT))
        tok_c3 = p.op("dve", lambda E: E.memset(consts[:, 3:4], 1.0))
        p.op("dve", lambda E: E.memset(consts[:, 4:5], 0.5))
        p.op("dve", lambda E: E.memset(consts[:, 5:6], -0.5))
        p.op("dve", lambda E: E.memset(consts[:, 6:7], -1.0))
        tok_ones = p.op("dve", lowp(nc, lambda E: E.tensor_copy(
            ones_r[:], bass.AP(consts, 3, [[8, 128], [0, 64]]))), after=[tok_c3])
        tok_consts = tok_ones

        def load_bias(dst, src, n):
            return p.dma("pool", lambda E: E.dma_start(
                out=dst[:, 0:n], in_=bass.AP(src.tensor, 0, [[1, 128], [128, n]]),
                allow_slow_non_contiguous=True))

        tok_bias = [load_bias(bias_pa, bproj_a, NCH), load_bias(bias_pb, bproj_b, NCH),
                    load_bias(bias_1, b1d, NFH), load_bias(bias_2, b2d, NCH)]

        rot_stats = Rot(2)
        rot_xq = Rot(4)
        rot_y = Rot(2)
        rot_xnew = Rot(8)

        def emit_ln_qchunk(sub, src_dram, s_i, qc_i, gi, pLN, rotLN, yT, nyT,
                           rot_yT, extra_dep=()):
            l0 = (gi % 2) * 32
            u4s = []
            tile_info = []
            for tt in range(4):
                ti = gi * 4 + tt
                gt = (s_i * T + qc_i * QC + tt * 128) // 128
                r0 = gt * 128
                sl = ti % 4
                dep = rot_xq.war(ti) + src_dep(sub, gt) + list(extra_dep)
                tl = p.dma("sp", lambda E, sl=sl, r0=r0, src_dram=src_dram: E.dma_start(
                    out=xq[:, ts(sl, C)], in_=src_dram[r0:r0 + 128, :]), after=dep)
                rot_xq.wrote(ti, tl)
                ss = ti % 2
                a0 = ss * 4
                t1 = p.op("dve", lambda E, sl=sl, ss=ss: E.bn_stats(
                    out=stats[:, ss * 12:ss * 12 + 6], in_=xq[:, sl * C:sl * C + 512]),
                    after=[tl] + rot_stats.war(ti))
                t2 = p.op("dve", lambda E, sl=sl, ss=ss: E.bn_stats(
                    out=stats[:, ss * 12 + 6:ss * 12 + 12],
                    in_=xq[:, sl * C + 512:sl * C + 1024]), after=[tl])
                t3a = p.op("dve", lambda E, ss=ss, a0=a0: E.bn_aggr(
                    out=aggr[:, a0:a0 + 2], in_=stats[:, ss * 12:ss * 12 + 6]),
                    after=[t1])
                t3 = p.op("dve", lambda E, ss=ss, a0=a0: E.bn_aggr(
                    out=aggr[:, a0 + 2:a0 + 4], in_=stats[:, ss * 12 + 6:ss * 12 + 12]),
                    after=[t2])
                u1 = p.op("dve", lambda E, a0=a0, l0=l0, tt=tt: E.tensor_sub(
                    out=lnsc[:, l0 + 12 + tt:l0 + 13 + tt], in0=aggr[:, a0:a0 + 1],
                    in1=aggr[:, a0 + 2:a0 + 3]), after=[t3a, t3])
                u2 = p.op("dve", lambda E, l0=l0, tt=tt: E.tensor_mul(
                    out=lnsc[:, l0 + 16 + tt:l0 + 17 + tt],
                    in0=lnsc[:, l0 + 12 + tt:l0 + 13 + tt],
                    in1=lnsc[:, l0 + 12 + tt:l0 + 13 + tt]), after=[u1])
                u3 = p.op("dve", lambda E, a0=a0, l0=l0, tt=tt: E.tensor_add(
                    out=lnsc[:, l0 + 20 + tt:l0 + 21 + tt], in0=aggr[:, a0 + 1:a0 + 2],
                    in1=aggr[:, a0 + 3:a0 + 4]), after=[t3a, t3])
                u4 = p.op("dve", lambda E, l0=l0, tt=tt: E.scalar_tensor_tensor(
                    out=lnsc[:, l0 + tt:l0 + tt + 1],
                    in0=lnsc[:, l0 + 16 + tt:l0 + 17 + tt], scalar=0.5,
                    in1=lnsc[:, l0 + 20 + tt:l0 + 21 + tt],
                    op0=OP.mult, op1=OP.add), after=[u2, u3])
                u5 = p.op("dve", lambda E, a0=a0, l0=l0, tt=tt: E.tensor_scalar(
                    out=lnsc[:, l0 + 4 + tt:l0 + 5 + tt], in0=aggr[:, a0:a0 + 1],
                    scalar1=aggr[:, a0 + 2:a0 + 3], scalar2=0.5,
                    op0=OP.add, op1=OP.mult), after=[t3a, t3])
                rot_stats.wrote(ti, u5)
                u4s.append(u4)
                u4s.append(u5)
                tile_info.append((ti, tt, sl, tl))
            L1 = p.op("act", lambda E, l0=l0: E.activation(
                out=lnsc[:, l0 + 24:l0 + 28], in_=lnsc[:, l0:l0 + 4],
                func=AF.Ln, bias=consts[:, 0:1], scale=consts[:, 4:5]),
                after=u4s + [tok_consts])
            L2 = p.op("act", lambda E, l0=l0: E.activation(
                out=lnsc[:, l0 + 8:l0 + 12], in_=lnsc[:, l0 + 24:l0 + 28],
                func=AF.Exp, scale=consts[:, 5:6]), after=[L1])
            last_ev = None
            for (ti, tt, sl, tl) in tile_info:
                ss = ti % 2
                t7 = p.op("dve", lambda E, sl=sl, ss=ss, l0=l0, tt=tt: E.tensor_scalar(
                    out=ybuf[:, ts(ss, C)], in0=xq[:, ts(sl, C)],
                    scalar1=lnsc[:, l0 + 4 + tt:l0 + 5 + tt],
                    scalar2=lnsc[:, l0 + 8 + tt:l0 + 9 + tt],
                    op0=OP.subtract, op1=OP.mult),
                    after=[L2, tl] + rot_y.war(ti))
                rot_y.wrote(ti, t7)
                rot_xq.read(ti, t7)
                pb = pLN[ti % 2]
                war0 = rotLN.war(ti) + ([last_ev] if last_ev else []) + rot_yT.war(gi)
                tr_last = None
                for cc in range(NCH):
                    tr_last = p.op("pe", lambda E, ss=ss, cc=cc, pb=pb: E.transpose(
                        pb[:, ts(cc, 128)],
                        ybuf[:, ss * C + cc * 128: ss * C + (cc + 1) * 128], ident[:]),
                        after=([t7, tok_ident] + war0 if cc == 0 else [t7]))
                rot_y.read(ti, tr_last)
                tev = p.op("dve", lambda E, tt=tt, gi=gi, pb=pb, yT=yT, nyT=nyT: E.tensor_copy(
                    bass.AP(yT, (gi % nyT) * NCH * QC + tt * 128,
                            [[nyT * NCH * QC, 128], [QC, NCH], [1, 128]]), pb[:]),
                    after=[tr_last])
                rotLN.wrote(ti, tev)
                last_ev = tev
            rot_yT.wrote(gi, last_ev)
            x_toks = [inf[3] for inf in tile_info]
            return last_ev, x_toks, gi % nyT

        def emit_wT(wdram, R, wT, wnat, rot_wnat, pLN, rotLN, base, extra=()):
            toks = []
            for rb in range(R // 128):
                tl = p.dma("pool", lambda E, rb=rb, wdram=wdram, base=base: E.dma_start(
                    out=wnat[:, ts(rb % 2, C)],
                    in_=wdram[base + rb * 128: base + (rb + 1) * 128, :]),
                    after=rot_wnat.war(rb) + list(extra))
                rot_wnat.wrote(rb, tl)
                pb = pLN[rb % 2]
                tr_last = None
                for cc in range(NCH):
                    tr_last = p.op("pe", lambda E, rb=rb, cc=cc, pb=pb: E.transpose(
                        pb[:, ts(cc, 128)],
                        wnat[:, (rb % 2) * C + cc * 128:(rb % 2) * C + (cc + 1) * 128],
                        ident[:]),
                        after=([tl, tok_ident] + rotLN.war(rb) if cc == 0 else [tl]))
                rot_wnat.read(rb, tr_last)
                tev = p.op("dve", lambda E, rb=rb, pb=pb, R=R, wT=wT: E.tensor_copy(
                    bass.AP(wT, rb * 128, [[NCH * R, 128], [R, NCH], [1, 128]]), pb[:]),
                    after=[tr_last])
                rotLN.wrote(rb, tev)
                toks.append(tev)
            return toks

        def emit_proj_add(sub, s_i, qc_i, wT, biasv, outf, out_dep, w_dep,
                          src, dst, PB):
            (pPRJ, rotPRJ, pTRb, rotTR, outsb, rot_outsb) = PB
            x_tok = {}
            last_prj_mm = None
            for w in range(2):
                ev_toks = {}
                for j in range(4):
                    oc = w * 4 + j
                    mi = ((s_i * NQC + qc_i) * 2 + w) * 4 + j
                    mm_last = None
                    for cc in range(NCH):
                        mm_last = p.op("pe", lambda E, oc=oc, cc=cc: E.matmul(
                            pPRJ[:], wT[:, cc * C + oc * 128: cc * C + (oc + 1) * 128],
                            outf[:, ts(cc, QC)], start=(cc == 0), stop=(cc == NCH - 1),
                            skip_group_check=True),
                            after=(rotPRJ.war(mi) + out_dep + w_dep if cc == 0 else []))
                    last_prj_mm = mm_last
                    ev = p.op("act", lambda E, oc=oc, j=j: E.activation(
                        out=outsb[:, ts(j, QC)], in_=pPRJ[:], func=AF.Identity,
                        bias=biasv[:, oc:oc + 1]),
                        after=[mm_last] + rot_outsb.war(mi) + tok_bias)
                    rotPRJ.wrote(mi, ev)
                    rot_outsb.wrote(mi, ev)
                    ev_toks[j] = (ev, mi)
                for tt in range(4):
                    ji = (s_i * NQC + qc_i) * 4 + tt
                    ii = ((s_i * NQC + qc_i) * 2 + w) * 4 + tt
                    gt = (s_i * T + qc_i * QC + tt * 128) // 128
                    r0 = gt * 128
                    bank = pTRb[tt % 2]
                    trs = []
                    for j in range(4):
                        ev, mi = ev_toks[j]
                        tr = p.op("pe", lambda E, j=j, tt=tt, bank=bank: E.transpose(
                            bank[:, ts(j, 128)],
                            outsb[:, j * QC + tt * 128: j * QC + (tt + 1) * 128],
                            ident_f[:]),
                            after=[ev, tok_identf] + (rotTR.war(ii) if j == 0 else []))
                        rot_outsb.read(mi, tr)
                        trs.append(tr)
                    if w == 0:
                        xi = 2000 * sub + ji
                        lx = p.dma("sp", lambda E, r0=r0, xi=xi: E.dma_start(
                            out=xq[:, ts(xi % 4, C)], in_=src[r0:r0 + 128, :]),
                            after=rot_xq.war(xi) + src_dep(sub, gt))
                        rot_xq.wrote(xi, lx)
                        x_tok[tt] = (lx, xi)
                    lx, xi = x_tok[tt]
                    xslot = (ji % 2) * 4 + w * 2
                    ad = p.op("dve", lambda E, ji=ji, xi=xi, w=w, bank=bank: E.tensor_add(
                        out=xnew[:, (ji % 2) * C + w * QC: (ji % 2) * C + (w + 1) * QC],
                        in0=bank[:], in1=xq[:, (xi % 4) * C + w * QC: (xi % 4) * C + (w + 1) * QC]),
                        after=trs + [lx] + rot_xnew.war(xslot) + rot_xnew.war(xslot + 1))
                    rot_xq.read(xi, ad)
                    rotTR.wrote(ii, ad)
                    sx = p.dma("sp", lambda E, r0=r0, ji=ji, w=w: E.dma_start(
                        out=dst[r0:r0 + 128, w * QC:(w + 1) * QC],
                        in_=xnew[:, (ji % 2) * C + w * QC: (ji % 2) * C + (w + 1) * QC]),
                        after=[ad])
                    rot_xnew.wrote(xslot, ad)
                    rot_xnew.wrote(xslot + 1, ad)
                    rot_xnew.read(xslot, sx)
                    rot_xnew.read(xslot + 1, sx)
                    xstore[(sub, gt, w)] = sx
            return last_prj_mm

        def attn_sublayer(sub, kind):
            src, dst = xsrc(sub), xdst(sub)
            subf = p.fence()
            wqkv = wqkv_a if kind == "a" else wqkv_b
            wproj = wproj_a if kind == "a" else wproj_b
            biasp = bias_pa if kind == "a" else bias_pb
            with ExitStack() as st:
                ee = st.enter_context
                wqkvT = ee(nc.sbuf_tensor(f"wqkvT{sub}", [128, NCH * 3 * C], BF16))
                yT = ee(nc.sbuf_tensor(f"yT{sub}", [128, 2 * NCH * QC], BF16))
                rot_yT = Rot(2)
                wprojT = ee(nc.sbuf_tensor(f"wprojT{sub}", [128, NCH * C], BF16))
                scr2 = ee(nc.sbuf_tensor(f"scr2_{sub}", [128, (2 if kind == "a" else 1) * QC], F32))
                outf = ee(nc.sbuf_tensor(f"outf{sub}", [128, NCH * QC], BF16))
                outsb = ee(nc.sbuf_tensor(f"outsb{sub}", [128, 4 * QC], F32))
                reps = ee(nc.sbuf_tensor(f"reps{sub}", [128, 2 * QC], F32))
                rrow = ee(nc.sbuf_tensor(f"rrow{sub}", [128, QC], F32R))
                rrowf = ee(nc.sbuf_tensor(f"rrowf{sub}", [128, QC], F32))
                if kind == "a":
                    scr1 = ee(nc.sbuf_tensor(f"scr1_{sub}", [128, 2 * QC], F32))
                    scr0 = ee(nc.sbuf_tensor(f"scr0_{sub}", [128, 2 * QC], F32))
                    rot_scr0 = Rot(2)
                    q_feat = ee(nc.sbuf_tensor(f"qf{sub}", [128, NCH * T], BF16))
                    kvacc = ee(nc.sbuf_tensor(f"kvacc{sub}", [128, 16 * 65], F32))
                    kvsb = ee(nc.sbuf_tensor(f"kvsb{sub}", [128, 8 * 65], BF16))
                    ktok = ee(nc.sbuf_tensor(f"ktok{sub}", [128, 2 * C], BF16))
                    vtok = ee(nc.sbuf_tensor(f"vtok{sub}", [128, 2 * H * 65], BF16))
                else:
                    vtok = ee(nc.sbuf_tensor(f"vtok{sub}", [128, NTT * H * 65], BF16))
                    expb = ee(nc.sbuf_tensor(f"expb{sub}", [128, 4 * QC], BF16))
                    avf = ee(nc.sbuf_tensor(f"avf{sub}", [128, 2 * QC], F32))
                    qstage = expb
                    kaug = ee(nc.sbuf_tensor(f"kaug{sub}", [128, 2 * T], BF16))
                    qsq = ee(nc.sbuf_tensor(f"qsq{sub}", [128, QC], BF16))
                    qaug = ee(nc.sbuf_tensor(f"qaug{sub}", [128, 2 * QC], BF16))
                    ksq = ee(nc.sbuf_tensor(f"ksq{sub}", [128, T], BF16))
                    ones_bf = ee(nc.sbuf_tensor(f"onesb{sub}", [128, 1], BF16))
                    nrm = ee(nc.sbuf_tensor(f"nrm{sub}", [128, 32], F32))

                pLNf = [ee(nc.psum_tensor(f"pLN{sub}_{i}", [128, QC], F32))
                        for i in range(2)]
                pLN = [t[:].bitcast(BF16) for t in pLNf]
                if kind == "a":
                    pMM = [ee(nc.psum_tensor(f"pMM{sub}_{i}", [128, QC], F32))
                           for i in range(2)]
                    pKV = [ee(nc.psum_tensor(f"pKV{sub}_{i}", [128, QC], F32))
                           for i in range(4)]
                    pREPs = [pKV[0], pKV[2]]
                    pPRJ = pKV[1]
                else:
                    pSC = [ee(nc.psum_tensor(f"pSC{sub}_{i}", [128, 2 * QC], F32))
                           for i in range(2)]
                    pMM = [pSC[0][:, 0:QC], pSC[1][:, 0:QC]]
                    pKV = [ee(nc.psum_tensor(f"pKV{sub}_{i}", [128, QC], F32))
                           for i in range(2)]
                    pPRJ = pKV[1]
                pTRb = [t[:] for t in pLNf]

                rotLN = Rot(2)
                rotMM = Rot(2)
                rotKV = Rot(4) if kind == "a" else Rot(2)
                rotREP = Rot(len(pREPs)) if kind == "a" else Rot(1)

                rotPRJ = Rot(1)
                rotTR = Rot(2)
                rot_outsb = Rot(4)
                rot_scr2 = Rot(2)
                rot_reps = Rot(2)
                rot_rrow = Rot(3)
                rot_outf = Rot(1)
                PB = (pPRJ, rotPRJ, pTRb, rotTR, outsb, rot_outsb)

                wnat = (q_feat if kind == "a" else ksq)
                rot_wnat = Rot(2)
                wq_toks = emit_wT(wqkv, 3 * C, wqkvT, wnat, rot_wnat, pLN, rotLN, 0,
                                  extra=subf + (tok_bias if sub == 0 else []))
                wp_toks = emit_wT(wproj, C, wprojT, wnat, rot_wnat, pLN, rotLN, 0)
                wq_done = wq_toks[-1]
                wp_done = wp_toks[-1]

                if kind == "a":
                    von = [p.op("dve", lambda E: E.memset(vtok[:], 1.0), after=subf)] * 2
                    rot_scr1 = Rot(2)
                    rot_ktok = Rot(2)
                    rot_vtok = Rot(2)
                    rot_kvacc = Rot(1)
                    rot_kvsb = Rot(1)
                else:
                    von_tok = p.op("dve", lambda E: E.memset(vtok[:], 1.0), after=subf)
                    kneg = p.op("dve", lambda E: E.memset(kaug[64:65, :], -1.0),
                                after=subf)
                    tko = p.op("dve", lambda E: E.memset(ones_bf[:], 1.0), after=subf)
                    rot_qstage = Rot(2)
                    rot_exp = Rot(2)
                    rot_vtok = Rot(NTT)
                    rot_kaug = Rot(2)
                    rot_qsq = Rot(1)
                    rotSC = Rot(2)
                    rot_avf = Rot(2)
                    rot_qaug = Rot(2)
                    rot_ksq = Rot(1)
                    rot_nrm = Rot(1)

                sample_fence = subf
                for s_i in range(S):
                    if kind == "a":
                        kvz = p.op("dve", lambda E: E.memset(kvacc[:], 0.0),
                                   after=rot_kvacc.war(s_i) + sample_fence)
                    for qc_i in range(NQC):
                        gi = s_i * NQC + qc_i
                        yT_tok, _, yT_sl = emit_ln_qchunk(
                            sub, src, s_i, qc_i, gi, pLN, rotLN, yT, 2, rot_yT,
                            extra_dep=(sample_fence if qc_i == 0 else ()))
                        if dbg and sub == 0 and gi == 0:
                            dtok = p.dma("sp", lambda E: E.dma_start(
                                out=dbg_yT, in_=yT[:, 0:NCH * QC]), after=[yT_tok])
                            rot_yT.read(gi, dtok)
                        ydep = [yT_tok, wq_done]

                        for ec in range(NCH):
                            mi = gi * 40 + ec
                            mm_last = None
                            for cc in range(NCH):
                                mm_last = p.op("pe", lambda E, ec=ec, cc=cc, mi=mi,
                                               yT_sl=yT_sl: E.matmul(
                                    pMM[mi % 2][:],
                                    wqkvT[:, cc * 3 * C + ec * 128: cc * 3 * C + (ec + 1) * 128],
                                    yT[:, yT_sl * NCH * QC + cc * QC: yT_sl * NCH * QC + (cc + 1) * QC],
                                    start=(cc == 0), stop=(cc == NCH - 1),
                                    skip_group_check=True),
                                    after=(rotMM.war(mi) + ydep if cc == 0 else []))
                            rot_yT.read(gi, mm_last)
                            qcol = ec * T + qc_i * QC
                            if kind == "a":
                                e0 = p.op("act", lambda E, mi=mi: E.activation(
                                    out=scr0[:, ts(mi % 2, QC)], in_=pMM[mi % 2][:],
                                    func=AF.Identity),
                                    after=[mm_last] + rot_scr0.war(mi))
                                rotMM.wrote(mi, e0)
                                e1 = p.op("act", lambda E, mi=mi: E.activation(
                                    out=scr1[:, ts(mi % 2, QC)], in_=scr0[:, ts(mi % 2, QC)],
                                    func=AF.Relu, scale=consts[:, 6:7]),
                                    after=[e0, tok_consts] + rot_scr1.war(mi))
                                e2 = p.op("act", lambda E, mi=mi: E.activation(
                                    out=scr2[:, ts(mi % 2, QC)], in_=scr1[:, ts(mi % 2, QC)],
                                    func=AF.Exp, scale=consts[:, 6:7]),
                                    after=[e1] + rot_scr2.war(mi))
                                e3 = p.op("dve", lambda E, mi=mi, qcol=qcol: E.scalar_tensor_tensor(
                                    out=q_feat[:, qcol:qcol + QC],
                                    in0=scr0[:, ts(mi % 2, QC)],
                                    scalar=0.0, in1=scr2[:, ts(mi % 2, QC)],
                                    op0=OP.max, op1=OP.add), after=[e2])
                                rot_scr0.wrote(mi, e3)
                                rot_scr1.wrote(mi, e1)
                                rot_scr2.wrote(mi, e3)
                            else:
                                e3 = p.op("act", lambda E, mi=mi: E.mul(
                                    out=qstage[:, ts(mi % 2, QC)], in_=pMM[mi % 2][:],
                                    mul=SCALE), after=[mm_last] + rot_qstage.war(mi))
                                rotMM.wrote(mi, e3)
                                sq = p.dma("sp", lambda E, mi=mi, ec=ec, qc_i=qc_i: E.dma_start(
                                    out=qspill[ec * 128:(ec + 1) * 128,
                                               qc_i * QC:(qc_i + 1) * QC],
                                    in_=qstage[:, ts(mi % 2, QC)]), after=[e3])
                                rot_qstage.wrote(mi, e3)
                                rot_qstage.read(mi, sq)
                                xstore[("q", sub, s_i, ec, qc_i)] = sq

                        if kind == "a":
                            for tt in range(4):
                                ti = gi * 4 + tt
                                tcol = yT_sl * NCH * QC + tt * 128
                                kd = None
                                for eb in range(2):
                                    mi = gi * 40 + 8 + tt * 4 + eb
                                    mm_last = None
                                    for cc in range(NCH):
                                        mm_last = p.op("pe", lambda E, cc=cc, eb=eb, mi=mi,
                                                       tcol=tcol: E.matmul(
                                            pMM[mi % 2][:],
                                            yT[:, cc * QC + tcol: cc * QC + tcol + 128],
                                            wqkvT[:, cc * 3 * C + C + eb * QC: cc * 3 * C + C + (eb + 1) * QC],
                                            start=(cc == 0), stop=(cc == NCH - 1),
                                            skip_group_check=True),
                                            after=(rotMM.war(mi) + [yT_tok, wq_done]
                                                   if cc == 0 else []))
                                    rot_yT.read(gi, mm_last)
                                    e0 = p.op("act", lambda E, mi=mi: E.activation(
                                        out=scr0[:, ts(mi % 2, QC)], in_=pMM[mi % 2][:],
                                        func=AF.Identity),
                                        after=[mm_last] + rot_scr0.war(mi))
                                    rotMM.wrote(mi, e0)
                                    e1 = p.op("act", lambda E, mi=mi: E.activation(
                                        out=scr1[:, ts(mi % 2, QC)], in_=scr0[:, ts(mi % 2, QC)],
                                        func=AF.Relu, scale=consts[:, 6:7]),
                                        after=[e0, tok_consts] + rot_scr1.war(mi))
                                    e2 = p.op("act", lambda E, mi=mi: E.activation(
                                        out=scr2[:, ts(mi % 2, QC)],
                                        in_=scr1[:, ts(mi % 2, QC)], func=AF.Exp,
                                        scale=consts[:, 6:7]),
                                        after=[e1] + rot_scr2.war(mi))
                                    e3 = p.op("dve", lambda E, mi=mi, ti=ti, eb=eb: E.scalar_tensor_tensor(
                                        out=ktok[:, (ti % 2) * C + eb * QC:(ti % 2) * C + (eb + 1) * QC],
                                        in0=scr0[:, ts(mi % 2, QC)], scalar=0.0,
                                        in1=scr2[:, ts(mi % 2, QC)], op0=OP.max, op1=OP.add),
                                        after=[e2] + (rot_ktok.war(ti) if eb == 0 else []))
                                    rot_scr0.wrote(mi, e3)
                                    rot_scr1.wrote(mi, e1)
                                    rot_scr2.wrote(mi, e3)
                                    kd = e3
                                rot_ktok.wrote(ti, kd)
                                vd = None
                                for eb in range(2):
                                    mi = gi * 40 + 8 + tt * 4 + 2 + eb
                                    mm_last = None
                                    for cc in range(NCH):
                                        mm_last = p.op("pe", lambda E, cc=cc, eb=eb, mi=mi,
                                                       tcol=tcol: E.matmul(
                                            pMM[mi % 2][:],
                                            yT[:, cc * QC + tcol: cc * QC + tcol + 128],
                                            wqkvT[:, cc * 3 * C + 2 * C + eb * QC: cc * 3 * C + 2 * C + (eb + 1) * QC],
                                            start=(cc == 0), stop=(cc == NCH - 1),
                                            skip_group_check=True),
                                            after=(rotMM.war(mi) + [yT_tok, wq_done]
                                                   if cc == 0 else []))
                                    rot_yT.read(gi, mm_last)
                                    ev = p.op("dve", lambda E, mi=mi, ti=ti, eb=eb: E.tensor_copy(
                                        bass.AP(vtok, (ti % 2) * H * 65 + eb * 8 * 65,
                                                [[2 * H * 65, 128], [65, 8], [1, 64]]),
                                        pMM[mi % 2][:]),
                                        after=[mm_last] + (rot_vtok.war(ti) if eb == 0 else [])
                                              + ([von[ti % 2]] if ti < 2 else []))
                                    rotMM.wrote(mi, ev)
                                    vd = ev
                                rot_vtok.wrote(ti, vd)
                                mmkv_last = {}
                                for h in range(H):
                                    slot = h % 4
                                    bk = pKV[h // 4]
                                    mmkv = p.op("pe", lambda E, h=h, bk=bk,
                                                slot=slot, ti=ti: E.matmul(
                                        bk[0:64, ts(slot, 65)],
                                        ktok[:, (ti % 2) * C + h * 64:(ti % 2) * C + (h + 1) * 64],
                                        vtok[:, (ti % 2) * H * 65 + h * 65:(ti % 2) * H * 65 + (h + 1) * 65],
                                        start=(slot == 0), stop=(slot == 3),
                                        skip_group_check=True),
                                        after=[kd, vd] + (rotKV.war(h // 4) if slot == 0 else []))
                                    rot_ktok.read(ti, mmkv)
                                    rot_vtok.read(ti, mmkv)
                                    mmkv_last[h // 4] = mmkv
                                for b in range(4):
                                    ka = p.op("dve", lambda E, b=b: E.tensor_add(
                                        out=kvacc[0:64, b * 4 * 65:(b + 1) * 4 * 65],
                                        in0=kvacc[0:64, b * 4 * 65:(b + 1) * 4 * 65],
                                        in1=pKV[b][0:64, 0:4 * 65]),
                                        after=[mmkv_last[b], kvz])
                                    rotKV.wrote(b, ka)
                        else:
                            for ec in range(NCH):
                                mi = gi * 40 + 8 + ec
                                mm_last = None
                                for cc in range(NCH):
                                    mm_last = p.op("pe", lambda E, ec=ec, cc=cc, mi=mi,
                                                   yT_sl=yT_sl: E.matmul(
                                        pMM[mi % 2][:],
                                        wqkvT[:, cc * 3 * C + C + ec * 128: cc * 3 * C + C + (ec + 1) * 128],
                                        yT[:, yT_sl * NCH * QC + cc * QC: yT_sl * NCH * QC + (cc + 1) * QC],
                                        start=(cc == 0), stop=(cc == NCH - 1),
                                        skip_group_check=True),
                                        after=(rotMM.war(mi) + ydep if cc == 0 else []))
                                rot_yT.read(gi, mm_last)
                                ekv = p.op("dve", lambda E, mi=mi: E.tensor_copy(
                                    qstage[:, ts(mi % 2, QC)], pMM[mi % 2][:]),
                                    after=[mm_last] + rot_qstage.war(mi))
                                rotMM.wrote(mi, ekv)
                                sk = p.dma("sp", lambda E, mi=mi, ec=ec, qc_i=qc_i: E.dma_start(
                                    out=kspill[ec * 128:(ec + 1) * 128, qc_i * QC:(qc_i + 1) * QC],
                                    in_=qstage[:, ts(mi % 2, QC)]), after=[ekv])
                                rot_qstage.wrote(mi, ekv)
                                rot_qstage.read(mi, sk)
                                xstore[("k", sub, s_i, ec, qc_i)] = sk
                            for tt in range(4):
                                ti = gi * 4 + tt
                                tcol = yT_sl * NCH * QC + tt * 128
                                glob_t = qc_i * 4 + tt
                                vti = s_i * NTT + glob_t
                                for eb in range(2):
                                    mi = gi * 40 + 20 + tt * 2 + eb
                                    mm_last = None
                                    for cc in range(NCH):
                                        mm_last = p.op("pe", lambda E, cc=cc, eb=eb, mi=mi,
                                                       tcol=tcol: E.matmul(
                                            pMM[mi % 2][:],
                                            yT[:, cc * QC + tcol: cc * QC + tcol + 128],
                                            wqkvT[:, cc * 3 * C + 2 * C + eb * QC: cc * 3 * C + 2 * C + (eb + 1) * QC],
                                            start=(cc == 0), stop=(cc == NCH - 1),
                                            skip_group_check=True),
                                            after=(rotMM.war(mi) + [yT_tok, wq_done]
                                                   if cc == 0 else []))
                                    rot_yT.read(gi, mm_last)
                                    ev = p.op("dve", lambda E, mi=mi, glob_t=glob_t, eb=eb: E.tensor_copy(
                                        bass.AP(vtok, glob_t * H * 65 + eb * 8 * 65,
                                                [[NTT * H * 65, 128], [65, 8], [1, 64]]),
                                        pMM[mi % 2][:]),
                                        after=[mm_last]
                                              + (rot_vtok.war(vti) if eb == 0 else [])
                                              + ([von_tok] if s_i == 0 and qc_i == 0 else []))
                                    rotMM.wrote(mi, ev)
                                    vd = ev
                                rot_vtok.wrote(vti, vd)

                    p1f = p.fence()
                    if kind == "a":
                        kv_last = [rotKV.wtok(b) for b in range(4)]
                        tkv = None
                        for h in range(H):
                            par, pp = h % 2, h // 2
                            tkv = p.op("dve", lambda E, h=h, par=par, pp=pp: E.tensor_copy(
                                kvsb[par * 64: par * 64 + 64, ts(pp, 65)],
                                kvacc[0:64, h * 65:(h + 1) * 65]),
                                after=kv_last + rot_kvsb.war(s_i))
                        rot_kvacc.wrote(s_i, tkv)
                        kv_dep = [tkv]
                        if dbg and sub == 0 and s_i == 0:
                            p.dma("sp", lambda E: E.dma_start(
                                out=dbg_kv, in_=kvacc[:]), after=[tkv])
                            dqt = p.dma("sp", lambda E: E.dma_start(
                                out=dbg_q, in_=q_feat[:]), after=p1f)
                        for qc_i in range(NQC):
                            out_toks = []
                            for h in range(H):
                                par = h % 2
                                pp = h // 2
                                mi = (s_i * NQC + qc_i) * H + h
                                qcol = pp * T + qc_i * QC
                                mo = p.op("pe", lambda E, par=par, pp=pp, mi=mi, qcol=qcol: E.matmul(
                                    pMM[mi % 2][0:65, :],
                                    kvsb[par * 64: par * 64 + 64, ts(pp, 65)],
                                    q_feat[par * 64: par * 64 + 64, qcol: qcol + QC],
                                    start=True, stop=True, skip_group_check=True),
                                    after=rotMM.war(mi) + kv_dep + p1f)
                                rot_kvsb.read(s_i, mo)
                                d1 = p.op("dve", lambda E, mi=mi: E.tensor_scalar_add(
                                    out=scr1[64:65, ts(mi % 2, QC)], in0=pMM[mi % 2][64:65, :],
                                    scalar1=AEPS), after=[mo] + rot_scr1.war(mi))
                                d2 = p.op("dve", lowp(nc, lambda E, mi=mi: E.reciprocal(
                                    out=rrow[64:65, :], in_=scr1[64:65, ts(mi % 2, QC)])),
                                    after=[d1] + rot_rrow.war(0))
                                rot_scr1.wrote(mi, d2)
                                pR = pREPs[h % len(pREPs)]
                                rs = h % 2
                                d3 = p.op("pe", lowp(nc, lambda E, pR=pR: E.matmul(
                                    pR[0:64, :], ones_r[64:65, :], rrow[64:65, :],
                                    start=True, stop=True, skip_group_check=True)),
                                    after=[d2, tok_ones] + rotREP.war(h))
                                rot_rrow.wrote(0, d3)
                                d4 = p.op("act", lambda E, pR=pR, rs=rs: E.copy(
                                    out=reps[0:64, ts(rs, QC)], in_=pR[0:64, :]),
                                    after=[d3] + rot_reps.war(rs))
                                rotREP.wrote(h, d4)
                                rot_reps.wrote(rs, d4)
                                ocol = pp * QC
                                if par == 0:
                                    d5 = p.op("dve", lambda E, mi=mi, ocol=ocol, rs=rs: E.tensor_tensor(
                                        out=outf[0:64, ocol: ocol + QC],
                                        in0=pMM[mi % 2][0:64, :], in1=reps[0:64, ts(rs, QC)],
                                        op=OP.mult), after=[d4] + rot_outf.war(0))
                                    rotMM.wrote(mi, d5)
                                    rot_reps.read(rs, d5)
                                    out_toks.append(d5)
                                else:
                                    d5 = p.op("dve", lambda E, mi=mi, rs=rs: E.tensor_tensor(
                                        out=scr2[0:64, ts(mi % 2, QC)],
                                        in0=pMM[mi % 2][0:64, :], in1=reps[0:64, ts(rs, QC)],
                                        op=OP.mult), after=[d4] + rot_scr2.war(mi))
                                    d6 = p.op("dve", lambda E, mi=mi, ocol=ocol: E.tensor_copy(
                                        outf[64:128, ocol: ocol + QC],
                                        scr2[0:64, ts(mi % 2, QC)]),
                                        after=[d5] + rot_outf.war(0))
                                    rotMM.wrote(mi, d5)
                                    rot_scr2.wrote(mi, d6)
                                    rot_reps.read(rs, d5)
                                    out_toks.append(d6)
                            if dbg and sub == 0 and s_i == 0 and qc_i == 0:
                                dot = p.dma("sp", lambda E: E.dma_start(
                                    out=dbg_outf, in_=outf[:]), after=out_toks)
                                rot_outf.read(0, dot)
                            rot_outf.wrote(0, out_toks[-1])
                            lpm = emit_proj_add(sub, s_i, qc_i, wprojT, biasp, outf,
                                                out_toks, [wp_done], src, dst, PB)
                            rot_outf.read(0, lpm)
                    else:
                        maxk_tok = {}
                        for qc_i in range(NQC):
                            out_toks = []
                            for h in range(H):
                                par = h % 2
                                pp = h // 2
                                hh = (s_i * NQC + qc_i) * H + h
                                lk = p.dma("sp", lambda E, h=h, hh=hh: E.dma_start(
                                    out=kaug[0:64, ts(hh % 2, T)],
                                    in_=kspill[h * 64:(h + 1) * 64, :]),
                                    after=rot_kaug.war(hh) + p1f
                                          + [xstore[("k", sub, s_i, pp, qq)] for qq in range(NQC)])
                                lq = p.dma("sp", lambda E, h=h, hh=hh, qc_i=qc_i: E.dma_start(
                                    out=qaug[0:64, ts(hh % 2, QC)],
                                    in_=qspill[h * 64:(h + 1) * 64, qc_i * QC:(qc_i + 1) * QC]),
                                    after=rot_qaug.war(hh) + p1f
                                          + [xstore[("q", sub, s_i, pp, qc_i)]])
                                if qc_i == 0:
                                    sqk = p.op("act", lambda E, hh=hh: E.activation(
                                        out=ksq[0:64, :], in_=kaug[0:64, ts(hh % 2, T)],
                                        func=AF.Square),
                                        after=[lk] + rot_ksq.war(0))
                                    rds = []
                                    for j in range(4):
                                        mmn = p.op("pe", lambda E, j=j: E.matmul(
                                            pPRJ[0:1, :], ones_bf[0:64, 0:1],
                                            ksq[0:64, j * QC:(j + 1) * QC],
                                            start=True, stop=True, skip_group_check=True),
                                            after=[sqk, tko] + rotPRJ.war(0))
                                        rd = p.op("dve", lambda E, j=j: E.reduce_max(
                                            out=nrm[0:1, j:j + 1], in_=pPRJ[0:1, :],
                                            axis=mybir.AxisListType.X),
                                            after=[mmn] + rot_nrm.war(0))
                                        rotPRJ.wrote(0, rd)
                                        rds.append(rd)
                                        rot_ksq.read(0, mmn)
                                    rd2 = p.op("dve", lambda E: E.reduce_max(
                                        out=nrm[0:1, 8:9], in_=nrm[0:1, 0:4],
                                        axis=mybir.AxisListType.X), after=rds)
                                    mkl = p.op("act", lambda E: E.activation(
                                        out=nrm[0:1, 9:10], in_=nrm[0:1, 8:9],
                                        func=AF.Ln), after=[rd2])
                                    mk = p.op("act", lambda E, h=h: E.activation(
                                        out=nrm[0:1, 16 + h:17 + h], in_=nrm[0:1, 9:10],
                                        func=AF.Exp, scale=consts[0:1, 4:5]),
                                        after=[mkl, tok_consts])
                                    rot_nrm.wrote(0, mk)
                                    maxk_tok[h] = mk
                                q2 = p.op("act", lambda E, hh=hh: E.activation(
                                    out=qsq[0:64, 0:QC], in_=qaug[0:64, ts(hh % 2, QC)],
                                    func=AF.Square), after=[lq] + rot_qsq.war(0))
                                mmq = p.op("pe", lambda E: E.matmul(
                                    pPRJ[0:1, :], ones_bf[0:64, 0:1], qsq[0:64, 0:QC],
                                    start=True, stop=True, skip_group_check=True),
                                    after=[q2, tko] + rotPRJ.war(0))
                                rot_qsq.wrote(0, q2)
                                rot_qsq.read(0, mmq)
                                qnl = p.op("act", lambda E: E.activation(
                                    out=rrowf[0:1, :],
                                    in_=pPRJ[0:1, :], func=AF.Ln),
                                    after=[mmq] + rot_rrow.war(2))
                                qn = p.op("act", lambda E: E.activation(
                                    out=scr2[0:1, 0:QC], in_=rrowf[0:1, :], func=AF.Exp,
                                    scale=consts[0:1, 4:5]),
                                    after=[qnl, tok_consts] + rot_scr2.war(0))
                                rot_rrow.wrote(2, qn)
                                rotPRJ.wrote(0, qnl)
                                cr1 = p.op("dve", lowp(nc, lambda E, h=h: E.tensor_scalar_mul(
                                    out=rrow[0:1, :], in0=scr2[0:1, 0:QC],
                                    scalar1=nrm[0:1, 16 + h:17 + h])),
                                    after=[qn, maxk_tok[h]] + rot_rrow.war(1))
                                rot_scr2.wrote(0, cr1)
                                rot_nrm.read(0, cr1)
                                cr2 = p.op("dve", lowp(nc, lambda E, hh=hh: E.tensor_copy(
                                    qaug[64:65, ts(hh % 2, QC)], rrow[0:1, :])),
                                    after=[cr1])
                                rot_rrow.wrote(1, cr2)
                                def emit_msc(kc):
                                    pr = (kc // 2) % 2
                                    t = p.op("pe", lambda E, kc=kc, hh=hh, pr=pr: E.matmul(
                                        pSC[pr][:, (kc % 2) * QC:(kc % 2 + 1) * QC],
                                        kaug[0:65, (hh % 2) * T + kc * 128:(hh % 2) * T + (kc + 1) * 128],
                                        qaug[0:65, ts(hh % 2, QC)],
                                        start=True, stop=True, skip_group_check=True),
                                        after=(rotSC.war(kc // 2) if kc % 2 == 0 else [])
                                              + [cr2] + ([lk, kneg] if kc == 0 else []))
                                    return t

                                def emit_exp(kc2, mm1, mm2):
                                    pr = kc2 % 2
                                    ex = p.op("act", lambda E, pr=pr: E.activation(
                                        out=expb[:, ts(pr, 2 * QC)], in_=pSC[pr][:],
                                        func=AF.Exp),
                                        after=[mm1, mm2] + rot_exp.war(kc2))
                                    rotSC.wrote(kc2, ex)
                                    if dbg and s_i == 0 and qc_i == 0 and h == 0 and kc2 == 0:
                                        dex = p.dma("sp", lambda E: E.dma_start(
                                            out=dbg_expb, in_=expb[:, 0:QC]), after=[ex])
                                        rot_exp.read(kc2, dex)
                                    return ex

                                def emit_mav(kc, ex):
                                    pr = (kc // 2) % 2
                                    mav = p.op("pe", lambda E, h=h, kc=kc, pr=pr: E.matmul(
                                        pKV[0][0:65, :],
                                        vtok[:, kc * H * 65 + h * 65: kc * H * 65 + (h + 1) * 65],
                                        expb[:, pr * 2 * QC + (kc % 2) * QC: pr * 2 * QC + (kc % 2 + 1) * QC],
                                        start=(kc == 0), stop=(kc == NTT - 1),
                                        skip_group_check=True),
                                        after=[ex] + (rotKV.war(0) if kc == 0 else []))
                                    if kc % 2 == 1:
                                        rot_exp.wrote(kc // 2, mav)
                                    rot_vtok.read(s_i * NTT + kc, mav)
                                    return mav

                                mm1 = emit_msc(0)
                                mm2 = emit_msc(1)
                                ex_p = emit_exp(0, mm1, mm2)
                                av_last = None
                                for kc2 in range(1, NTT // 2):
                                    mm1 = emit_msc(2 * kc2)
                                    mm2 = emit_msc(2 * kc2 + 1)
                                    emit_mav(2 * kc2 - 2, ex_p)
                                    emit_mav(2 * kc2 - 1, ex_p)
                                    ex_p = emit_exp(kc2, mm1, mm2)
                                emit_mav(NTT - 2, ex_p)
                                av_last = emit_mav(NTT - 1, ex_p)
                                msc_p = mm2
                                rot_kaug.read(hh, msc_p)
                                rot_qaug.read(hh, msc_p)
                                rs = h % 2
                                e0 = p.op("dve", lambda E, rs=rs: E.tensor_copy(
                                    avf[0:65, ts(rs, QC)], pKV[0][0:65, :]),
                                    after=[av_last] + rot_avf.war(rs))
                                rotKV.wrote(0, e0)
                                rot_avf.wrote(rs, e0)
                                d2 = p.op("dve", lowp(nc, lambda E, rs=rs: E.reciprocal(
                                    out=rrow[64:65, :], in_=avf[64:65, rs * QC:(rs + 1) * QC])),
                                    after=[e0] + rot_rrow.war(0))
                                rot_avf.read(rs, d2)
                                d3 = p.op("pe", lowp(nc, lambda E: E.matmul(
                                    pSC[1][0:64, 0:QC], ones_r[64:65, :], rrow[64:65, :],
                                    start=True, stop=True, skip_group_check=True)),
                                    after=[d2, tok_ones] + rotSC.war(1))
                                rot_rrow.wrote(0, d3)
                                d4 = p.op("act", lambda E, rs=rs: E.copy(
                                    out=reps[0:64, ts(rs, QC)], in_=pSC[1][0:64, 0:QC]),
                                    after=[d3] + rot_reps.war(rs))
                                rotSC.wrote(1, d4)
                                rot_reps.wrote(rs, d4)
                                ocol = pp * QC
                                if par == 0:
                                    d5 = p.op("dve", lambda E, ocol=ocol, rs=rs: E.tensor_tensor(
                                        out=outf[0:64, ocol: ocol + QC],
                                        in0=avf[0:64, rs * QC:(rs + 1) * QC],
                                        in1=reps[0:64, ts(rs, QC)],
                                        op=OP.mult), after=[d4] + rot_outf.war(0))
                                    rot_avf.read(rs, d5)
                                    rot_reps.read(rs, d5)
                                    out_toks.append(d5)
                                else:
                                    d5 = p.op("dve", lambda E, rs=rs: E.tensor_tensor(
                                        out=scr2[0:64, 0:QC],
                                        in0=avf[0:64, rs * QC:(rs + 1) * QC],
                                        in1=reps[0:64, ts(rs, QC)],
                                        op=OP.mult), after=[d4] + rot_scr2.war(0))
                                    d6 = p.op("dve", lambda E, ocol=ocol: E.tensor_copy(
                                        outf[64:128, ocol: ocol + QC],
                                        scr2[0:64, 0:QC]),
                                        after=[d5] + rot_outf.war(0))
                                    rot_avf.read(rs, d5)
                                    rot_scr2.wrote(0, d6)
                                    rot_reps.read(rs, d5)
                                    out_toks.append(d6)
                            if dbg and s_i == 0 and qc_i == 0:
                                dob = p.dma("sp", lambda E: E.dma_start(
                                    out=dbg_outfb, in_=outf[:]), after=out_toks)
                                rot_outf.read(0, dob)
                            rot_outf.wrote(0, out_toks[-1])
                            lpm = emit_proj_add(sub, s_i, qc_i, wprojT, biasp, outf,
                                                out_toks, [wp_done], src, dst, PB)
                            rot_outf.read(0, lpm)
                    sample_fence = p.fence()

        def ffn_sublayer(sub):
            src, dst = xsrc(sub), xdst(sub)
            subf = p.fence()
            with ExitStack() as st:
                ee = st.enter_context
                w1sb = ee(nc.sbuf_tensor(f"w1sb{sub}", [128, NCH * FF], BF16))
                w2sb = ee(nc.sbuf_tensor(f"w2sb{sub}", [128, NFH * C], BF16))
                hbuf = ee(nc.sbuf_tensor(f"hbuf{sub}", [128, NFH * QC], BF16))
                yT = ee(nc.sbuf_tensor(f"yTF{sub}", [128, NCH * QC], BF16))
                rot_yT = Rot(1)
                outsb2 = ee(nc.sbuf_tensor(f"outsbF{sub}", [128, 2 * QC], F32))
                pLNf = [ee(nc.psum_tensor(f"pLNF{sub}_{i}", [128, QC], F32))
                        for i in range(2)]
                pLN = [t[:].bitcast(BF16) for t in pLNf]
                pTRb = [t[:] for t in pLNf]
                pH = [ee(nc.psum_tensor(f"pHF{sub}_{i}", [128, QC], F32)) for i in range(2)]
                pO = [ee(nc.psum_tensor(f"pOF{sub}_{i}", [128, QC], F32)) for i in range(2)]
                rotLN = Rot(2)
                rotH = Rot(2)
                rotO = Rot(2)
                rot_h = Rot(1)
                rot_outsb = Rot(2)

                w1_toks = [p.dma("pool", lambda E, cc=cc: E.dma_start(
                    out=w1sb[:, ts(cc, FF)], in_=w1d[cc * 128:(cc + 1) * 128, :]),
                    after=(subf if cc == 0 else []))
                    for cc in range(NCH)]
                w2_toks = [p.dma("pool", lambda E, hh=hh: E.dma_start(
                    out=w2sb[:, ts(hh, C)], in_=w2d[hh * 128:(hh + 1) * 128, :]))
                    for hh in range(NFH)]
                w1_done = w1_toks[-1]
                w2_done = w2_toks[-1]

                for s_i in range(S):
                    for qc_i in range(NQC):
                        gi = s_i * NQC + qc_i
                        yT_tok, x_toks, _ = emit_ln_qchunk(
                            sub, src, s_i, qc_i, gi, pLN, rotLN, yT, 1, rot_yT,
                            extra_dep=(subf if gi == 0 else ()))
                        if dbg and sub == 1 and gi == 0:
                            dt2 = p.dma("sp", lambda E: E.dma_start(
                                out=dbg_yT2, in_=yT[:]), after=[yT_tok])
                            rot_yT.read(gi, dt2)
                        h_last = None
                        first_fc_deps = rot_h.war(gi)
                        for fc in range(NFH):
                            mi = gi * 80 + fc
                            mm_last = None
                            for cc in range(NCH):
                                mm_last = p.op("pe", lambda E, fc=fc, cc=cc, mi=mi: E.matmul(
                                    pH[mi % 2][:],
                                    w1sb[:, cc * FF + fc * 128: cc * FF + (fc + 1) * 128],
                                    yT[:, ts(cc, QC)],
                                    start=(cc == 0), stop=(cc == NCH - 1),
                                    skip_group_check=True),
                                    after=(rotH.war(mi) + [yT_tok, w1_done]
                                           + (first_fc_deps if fc == 0 else [])
                                           if cc == 0 else []))
                            rot_yT.read(gi, mm_last)
                            gv = p.op("act", lambda E, fc=fc, mi=mi: E.activation(
                                out=hbuf[:, ts(fc, QC)], in_=pH[mi % 2][:], func=AF.Gelu,
                                bias=bias_1[:, fc:fc + 1]),
                                after=[mm_last] + tok_bias)
                            rotH.wrote(mi, gv)
                            h_last = gv
                        rot_h.wrote(gi, h_last)
                        if dbg and sub == 1 and gi == 0:
                            dh = p.dma("sp", lambda E: E.dma_start(
                                out=dbg_h, in_=hbuf[:]), after=[h_last])
                            rot_h.read(gi, dh)
                        ad_toks = {}
                        for w in range(4):
                            ev_toks = {}
                            for j in range(2):
                                oc = w * 2 + j
                                mi = (gi * 4 + w) * 2 + j
                                mm_last = None
                                for hh2 in range(NFH):
                                    mm_last = p.op("pe", lambda E, oc=oc, hh2=hh2, mi=mi: E.matmul(
                                        pO[mi % 2][:],
                                        w2sb[:, hh2 * C + oc * 128: hh2 * C + (oc + 1) * 128],
                                        hbuf[:, ts(hh2, QC)],
                                        start=(hh2 == 0), stop=(hh2 == NFH - 1),
                                        skip_group_check=True),
                                        after=(rotO.war(mi) + [h_last, w2_done]
                                               if hh2 == 0 else []))
                                rot_h.read(gi, mm_last)
                                ev = p.op("act", lambda E, oc=oc, j=j, mi=mi: E.activation(
                                    out=outsb2[:, ts(j, QC)], in_=pO[mi % 2][:],
                                    func=AF.Identity, bias=bias_2[:, oc:oc + 1]),
                                    after=[mm_last] + rot_outsb.war(mi) + tok_bias)
                                rotO.wrote(mi, ev)
                                rot_outsb.wrote(mi, ev)
                                if dbg and sub == 1 and gi == 0 and w == 0 and j == 1:
                                    dosb = p.dma("sp", lambda E: E.dma_start(
                                        out=dbg_osb, in_=outsb2[:]), after=[ev_toks[0][0], ev])
                                    rot_outsb.read(mi, dosb)
                                    rot_outsb.read(mi - 1, dosb)
                                ev_toks[j] = (ev, mi)
                            for tt in range(4):
                                ji = gi * 4 + tt
                                lnidx = gi * 16 + w * 4 + tt
                                bank = pTRb[tt % 2]
                                trs = []
                                for j in range(2):
                                    ev, mi = ev_toks[j]
                                    tr = p.op("pe", lambda E, j=j, tt=tt, bank=bank: E.transpose(
                                        bank[:, ts(j, 128)],
                                        outsb2[:, j * QC + tt * 128: j * QC + (tt + 1) * 128],
                                        ident_f[:]),
                                        after=[ev, tok_identf]
                                              + (rotLN.war(lnidx) if j == 0 else []))
                                    rot_outsb.read(mi, tr)
                                    trs.append(tr)
                                xi = gi * 4 + tt
                                xslot = (ji % 2) * 4 + w
                                gt = (s_i * T + qc_i * QC + tt * 128) // 128
                                r0 = gt * 128
                                ad = p.op("dve", lambda E, ji=ji, xi=xi, w=w, bank=bank: E.tensor_add(
                                    out=xnew[:, (ji % 2) * C + w * 256: (ji % 2) * C + (w + 1) * 256],
                                    in0=bank[:, 0:256],
                                    in1=xq[:, (xi % 4) * C + w * 256: (xi % 4) * C + (w + 1) * 256]),
                                    after=trs + [x_toks[tt]] + rot_xnew.war(xslot))
                                rot_xq.read(xi, ad)
                                rotLN.wrote(lnidx, ad)
                                rot_xnew.wrote(xslot, ad)
                                sx = p.dma("sp", lambda E, r0=r0, ji=ji, w=w: E.dma_start(
                                    out=dst[r0:r0 + 128, w * 256:(w + 1) * 256],
                                    in_=xnew[:, (ji % 2) * C + w * 256: (ji % 2) * C + (w + 1) * 256]),
                                    after=[ad])
                                rot_xnew.read(xslot, sx)
                                xstore[(sub, gt, "f%d" % w)] = sx

        seq = ["a", "f", "b", "f"]
        for sub in range(n_sub):
            if seq[sub] == "f":
                ffn_sublayer(sub)
            else:
                attn_sublayer(sub, seq[sub])

        with nc.Block() as block:
            p.emit(nc, block, stack)

    return nc


def kernel(L0_attn_a_x, L0_attn_a_Wqkv, L0_attn_a_Wproj, L0_attn_a_bproj,
           L0_attn_b_w_qkv, L0_attn_b_w_proj, L0_attn_b_b_proj,
           L0_ffn_w1, L0_ffn_b1, L0_ffn_w2, L0_ffn_b2, n_sub=4):
    x = np.ascontiguousarray(np.asarray(L0_attn_a_x, dtype=np.float32))
    B, T_, C_ = x.shape
    n = 8
    nc = build_program(n_sub=n_sub)
    ident = np.eye(128, dtype=ml_dtypes.bfloat16)
    base = {
        "wqkv_a": np.ascontiguousarray(np.asarray(L0_attn_a_Wqkv, dtype=np.float32)),
        "wproj_a": np.ascontiguousarray(np.asarray(L0_attn_a_Wproj, dtype=np.float32)),
        "bproj_a": np.ascontiguousarray(np.asarray(L0_attn_a_bproj, dtype=np.float32)),
        "wqkv_b": np.ascontiguousarray(np.asarray(L0_attn_b_w_qkv, dtype=np.float32)),
        "wproj_b": np.ascontiguousarray(np.asarray(L0_attn_b_w_proj, dtype=np.float32)),
        "bproj_b": np.ascontiguousarray(np.asarray(L0_attn_b_b_proj, dtype=np.float32)),
        "w1": np.ascontiguousarray(np.asarray(L0_ffn_w1, dtype=np.float32)),
        "b1": np.ascontiguousarray(np.asarray(L0_ffn_b1, dtype=np.float32)),
        "w2": np.ascontiguousarray(np.asarray(L0_ffn_w2, dtype=np.float32)),
        "b2": np.ascontiguousarray(np.asarray(L0_ffn_b2, dtype=np.float32)),
        "ident": ident,
    }
    in_maps = []
    for c in range(n):
        m = dict(base)
        m["xin"] = np.ascontiguousarray(x[c * S:(c + 1) * S].reshape(S * T_, C_))
        in_maps.append(m)
    res = run_bass_kernel_spmd(nc, in_maps, core_ids=list(range(n)))
    out = np.empty((B, T_, C_), dtype=np.float32)
    for c in range(n):
        out[c * S:(c + 1) * S] = np.asarray(res.results[c]["xout"]).reshape(S, T_, C_)
    return out


# version 11 (final)
# speedup vs baseline: 1.0936x; 1.0865x over previous
import numpy as np
import ml_dtypes
import concourse.bass as bass
import concourse.mybir as mybir
from concourse.bass_utils import run_bass_kernel_spmd
from contextlib import ExitStack

F32 = mybir.dt.float32
F32R = mybir.dt.float32r
BF16 = mybir.dt.bfloat16
AF = mybir.ActivationFunctionType
OP = mybir.AluOpType

S = 2
T = 2048
C = 1024
H = 16
HD = 64
FF = 4096
NCH = C // 128
NTT = T // 128
QC = 512
NQC = T // QC
NFH = FF // 128
LNEPS = 1e-5
AEPS = 1e-6
SCALE = 0.125
ESHIFT = -12.0

ENGINES = ("sp", "act", "dve", "pe", "pool")


class Plan:
    def __init__(self):
        self.ops = {e: [] for e in ENGINES}
        self.cnt = {e: 0 for e in ENGINES}

    def op(self, eng, fn, after=(), inc=1):
        waits = {}
        for tok in after:
            if tok is None:
                continue
            e2, v2 = tok
            if v2 > waits.get(e2, 0):
                waits[e2] = v2
        self.cnt[eng] += inc
        self.ops[eng].append((waits, fn, inc))
        return (eng, self.cnt[eng])

    def dma(self, eng, fn, after=()):
        return self.op(eng, fn, after, inc=16)

    def fence(self):
        return [(e, self.cnt[e]) for e in ENGINES if self.cnt[e] > 0]

    def emit(self, nc, block, stack):
        sems = {e: stack.enter_context(nc.semaphore(f"sem_{e}")) for e in ENGINES}
        engmap = {"sp": block.sync, "act": block.scalar, "dve": block.vector,
                  "pe": block.tensor, "pool": block.gpsimd}
        plan = self

        def mk(eng_name):
            def body(E):
                seen = {e: 0 for e in ENGINES}
                for waits, fn, inc in plan.ops[eng_name]:
                    for e2 in sorted(waits):
                        v2 = waits[e2]
                        if v2 > seen[e2]:
                            E.wait_ge(sems[e2], v2)
                            seen[e2] = v2
                    fn(E).then_inc(sems[eng_name], inc)
                if eng_name == "sp":
                    E.wait_ge(sems["sp"], plan.cnt["sp"])
                    if plan.cnt["pool"]:
                        E.wait_ge(sems["pool"], plan.cnt["pool"])
            return body

        for e in ENGINES:
            if plan.ops[e] or e == "sp":
                engmap[e](mk(e))


class Rot:

    def __init__(self, n):
        self.n = n
        self.wr = [None] * n
        self.rd = [[] for _ in range(n)]

    def war(self, i):
        s = i % self.n
        dep = list(self.rd[s])
        if self.wr[s] is not None:
            dep.append(self.wr[s])
        return dep

    def wrote(self, i, tok):
        s = i % self.n
        self.wr[s] = tok
        self.rd[s] = []

    def read(self, i, tok):
        self.rd[i % self.n].append(tok)

    def wtok(self, i):
        return self.wr[i % self.n]


def ts(i, sz):
    return slice(i * sz, (i + 1) * sz)


def lowp(nc, f):
    def g(E):
        with nc.allow_low_precision(reason="fp32r broadcast path"):
            return f(E)
    return g


def build_program(n_sub=4, dbg=False):
    nc = bass.Bass("TRN2", target_bir_lowering=False, debug=False)
    NT = S * T

    xin = nc.dram_tensor("xin", [NT, C], F32, kind="ExternalInput").ap()
    xout = nc.dram_tensor("xout", [NT, C], F32, kind="ExternalOutput").ap()
    wqkv_a = nc.dram_tensor("wqkv_a", [3 * C, C], F32, kind="ExternalInput").ap()
    wproj_a = nc.dram_tensor("wproj_a", [C, C], F32, kind="ExternalInput").ap()
    bproj_a = nc.dram_tensor("bproj_a", [C], F32, kind="ExternalInput").ap()
    wqkv_b = nc.dram_tensor("wqkv_b", [3 * C, C], F32, kind="ExternalInput").ap()
    wproj_b = nc.dram_tensor("wproj_b", [C, C], F32, kind="ExternalInput").ap()
    bproj_b = nc.dram_tensor("bproj_b", [C], F32, kind="ExternalInput").ap()
    w1d = nc.dram_tensor("w1", [C, FF], F32, kind="ExternalInput").ap()
    b1d = nc.dram_tensor("b1", [FF], F32, kind="ExternalInput").ap()
    w2d = nc.dram_tensor("w2", [FF, C], F32, kind="ExternalInput").ap()
    b2d = nc.dram_tensor("b2", [C], F32, kind="ExternalInput").ap()
    ident_in = nc.dram_tensor("ident", [128, 128], BF16, kind="ExternalInput").ap()

    xs = [nc.dram_tensor(f"xs{i}", [NT, C], F32, kind="Internal").ap() for i in range(2)]
    if dbg:
        dbg_yT = nc.dram_tensor("dbg_yT", [128, NCH * QC], BF16, kind="ExternalOutput").ap()
        dbg_q = nc.dram_tensor("dbg_q", [128, NCH * T], BF16, kind="ExternalOutput").ap()
        dbg_kv = nc.dram_tensor("dbg_kv", [128, 16 * 65], F32, kind="ExternalOutput").ap()
        dbg_outf = nc.dram_tensor("dbg_outf", [128, NCH * QC], BF16, kind="ExternalOutput").ap()
        dbg_prj = nc.dram_tensor("dbg_prj", [128, 4 * QC], F32, kind="ExternalOutput").ap()
        dbg_xnew = nc.dram_tensor("dbg_xnew", [128, C], F32, kind="ExternalOutput").ap()
        dbg_ptr = nc.dram_tensor("dbg_ptr", [128, QC], F32, kind="ExternalOutput").ap()
        dbg_yT2 = nc.dram_tensor("dbg_yT2", [128, NCH * QC], BF16, kind="ExternalOutput").ap()
        dbg_h = nc.dram_tensor("dbg_h", [128, NFH * QC], BF16, kind="ExternalOutput").ap()
        dbg_osb = nc.dram_tensor("dbg_osb", [128, 2 * QC], F32, kind="ExternalOutput").ap()
        dbg_outfb = nc.dram_tensor("dbg_outfb", [128, NCH * QC], BF16, kind="ExternalOutput").ap()
        dbg_expb = nc.dram_tensor("dbg_expb", [128, QC], BF16, kind="ExternalOutput").ap()
    qspill = nc.dram_tensor("qspill", [C, T], BF16, kind="Internal").ap()
    kspill = nc.dram_tensor("kspill", [C, T], BF16, kind="Internal").ap()

    def xsrc(sub):
        return xin if sub == 0 else xs[(sub - 1) % 2]

    def xdst(sub):
        return xout if sub == n_sub - 1 else xs[sub % 2]

    p = Plan()
    xstore = {}

    def src_dep(sub, gt):
        if sub == 0:
            return []
        return [xstore.get((sub - 1, gt, 0)), xstore.get((sub - 1, gt, 1)),
                xstore.get((sub - 1, gt, "f0")), xstore.get((sub - 1, gt, "f1")),
                xstore.get((sub - 1, gt, "f2")), xstore.get((sub - 1, gt, "f3")),
                xstore.get((sub - 1, gt))]

    with ExitStack() as stack:
        e = stack.enter_context

        ident = e(nc.sbuf_tensor("identb", [128, 128], BF16))
        ident_f = e(nc.sbuf_tensor("identf", [128, 128], F32))
        ones_r = e(nc.sbuf_tensor("ones_r", [128, 64], F32R))
        consts = e(nc.sbuf_tensor("consts", [128, 8], F32))
        bias_pa = e(nc.sbuf_tensor("bias_pa", [128, NCH], F32))
        bias_pb = e(nc.sbuf_tensor("bias_pb", [128, NCH], F32))
        bias_1 = e(nc.sbuf_tensor("bias_1", [128, NFH], F32))
        bias_2 = e(nc.sbuf_tensor("bias_2", [128, NCH], F32))
        stats = e(nc.sbuf_tensor("stats", [128, 2 * 12], F32))
        aggr = e(nc.sbuf_tensor("aggr", [128, 2 * 4], F32))
        lnsc = e(nc.sbuf_tensor("lnsc", [128, 2 * 32], F32))
        xq = e(nc.sbuf_tensor("xq", [128, 4 * C], F32))
        ybuf = e(nc.sbuf_tensor("ybuf", [128, 2 * C], BF16))
        xnew = e(nc.sbuf_tensor("xnew", [128, 2 * C], F32))

        tok_ident = p.dma("sp", lambda E: E.dma_start(out=ident[:], in_=ident_in))
        tok_identf = p.op("dve", lambda E: E.tensor_copy(ident_f[:], ident[:]),
                          after=[tok_ident])
        p.op("dve", lambda E: E.memset(consts[:, 0:1], LNEPS))
        p.op("dve", lambda E: E.memset(consts[:, 1:2], AEPS))
        p.op("dve", lambda E: E.memset(consts[:, 2:3], ESHIFT))
        tok_c3 = p.op("dve", lambda E: E.memset(consts[:, 3:4], 1.0))
        p.op("dve", lambda E: E.memset(consts[:, 4:5], 0.5))
        p.op("dve", lambda E: E.memset(consts[:, 5:6], -0.5))
        p.op("dve", lambda E: E.memset(consts[:, 6:7], -1.0))
        tok_ones = p.op("dve", lowp(nc, lambda E: E.tensor_copy(
            ones_r[:], bass.AP(consts, 3, [[8, 128], [0, 64]]))), after=[tok_c3])
        tok_consts = tok_ones

        def load_bias(dst, src, n):
            return p.dma("pool", lambda E: E.dma_start(
                out=dst[:, 0:n], in_=bass.AP(src.tensor, 0, [[1, 128], [128, n]]),
                allow_slow_non_contiguous=True))

        tok_bias = [load_bias(bias_pa, bproj_a, NCH), load_bias(bias_pb, bproj_b, NCH),
                    load_bias(bias_1, b1d, NFH), load_bias(bias_2, b2d, NCH)]

        rot_stats = Rot(2)
        rot_xq = Rot(4)
        rot_y = Rot(2)
        rot_xnew = Rot(8)

        def emit_ln_qchunk(sub, src_dram, s_i, qc_i, gi, pLN, rotLN, yT, nyT,
                           rot_yT, extra_dep=()):
            l0 = (gi % 2) * 32
            u4s = []
            tile_info = []
            for tt in range(4):
                ti = gi * 4 + tt
                gt = (s_i * T + qc_i * QC + tt * 128) // 128
                r0 = gt * 128
                sl = ti % 4
                dep = rot_xq.war(ti) + src_dep(sub, gt) + list(extra_dep)
                tl = p.dma("sp", lambda E, sl=sl, r0=r0, src_dram=src_dram: E.dma_start(
                    out=xq[:, ts(sl, C)], in_=src_dram[r0:r0 + 128, :]), after=dep)
                rot_xq.wrote(ti, tl)
                ss = ti % 2
                a0 = ss * 4
                t1 = p.op("dve", lambda E, sl=sl, ss=ss: E.bn_stats(
                    out=stats[:, ss * 12:ss * 12 + 6], in_=xq[:, sl * C:sl * C + 512]),
                    after=[tl] + rot_stats.war(ti))
                t2 = p.op("dve", lambda E, sl=sl, ss=ss: E.bn_stats(
                    out=stats[:, ss * 12 + 6:ss * 12 + 12],
                    in_=xq[:, sl * C + 512:sl * C + 1024]), after=[tl])
                t3a = p.op("dve", lambda E, ss=ss, a0=a0: E.bn_aggr(
                    out=aggr[:, a0:a0 + 2], in_=stats[:, ss * 12:ss * 12 + 6]),
                    after=[t1])
                t3 = p.op("dve", lambda E, ss=ss, a0=a0: E.bn_aggr(
                    out=aggr[:, a0 + 2:a0 + 4], in_=stats[:, ss * 12 + 6:ss * 12 + 12]),
                    after=[t2])
                u1 = p.op("dve", lambda E, a0=a0, l0=l0, tt=tt: E.tensor_sub(
                    out=lnsc[:, l0 + 12 + tt:l0 + 13 + tt], in0=aggr[:, a0:a0 + 1],
                    in1=aggr[:, a0 + 2:a0 + 3]), after=[t3a, t3])
                u2 = p.op("dve", lambda E, l0=l0, tt=tt: E.tensor_mul(
                    out=lnsc[:, l0 + 16 + tt:l0 + 17 + tt],
                    in0=lnsc[:, l0 + 12 + tt:l0 + 13 + tt],
                    in1=lnsc[:, l0 + 12 + tt:l0 + 13 + tt]), after=[u1])
                u3 = p.op("dve", lambda E, a0=a0, l0=l0, tt=tt: E.tensor_add(
                    out=lnsc[:, l0 + 20 + tt:l0 + 21 + tt], in0=aggr[:, a0 + 1:a0 + 2],
                    in1=aggr[:, a0 + 3:a0 + 4]), after=[t3a, t3])
                u4 = p.op("dve", lambda E, l0=l0, tt=tt: E.scalar_tensor_tensor(
                    out=lnsc[:, l0 + tt:l0 + tt + 1],
                    in0=lnsc[:, l0 + 16 + tt:l0 + 17 + tt], scalar=0.5,
                    in1=lnsc[:, l0 + 20 + tt:l0 + 21 + tt],
                    op0=OP.mult, op1=OP.add), after=[u2, u3])
                u5 = p.op("dve", lambda E, a0=a0, l0=l0, tt=tt: E.tensor_scalar(
                    out=lnsc[:, l0 + 4 + tt:l0 + 5 + tt], in0=aggr[:, a0:a0 + 1],
                    scalar1=aggr[:, a0 + 2:a0 + 3], scalar2=0.5,
                    op0=OP.add, op1=OP.mult), after=[t3a, t3])
                rot_stats.wrote(ti, u5)
                u4s.append(u4)
                u4s.append(u5)
                tile_info.append((ti, tt, sl, tl))
            L1 = p.op("act", lambda E, l0=l0: E.activation(
                out=lnsc[:, l0 + 24:l0 + 28], in_=lnsc[:, l0:l0 + 4],
                func=AF.Ln, bias=consts[:, 0:1], scale=consts[:, 4:5]),
                after=u4s + [tok_consts])
            L2 = p.op("act", lambda E, l0=l0: E.activation(
                out=lnsc[:, l0 + 8:l0 + 12], in_=lnsc[:, l0 + 24:l0 + 28],
                func=AF.Exp, scale=consts[:, 5:6]), after=[L1])
            last_ev = None
            for (ti, tt, sl, tl) in tile_info:
                ss = ti % 2
                t7 = p.op("dve", lambda E, sl=sl, ss=ss, l0=l0, tt=tt: E.tensor_scalar(
                    out=ybuf[:, ts(ss, C)], in0=xq[:, ts(sl, C)],
                    scalar1=lnsc[:, l0 + 4 + tt:l0 + 5 + tt],
                    scalar2=lnsc[:, l0 + 8 + tt:l0 + 9 + tt],
                    op0=OP.subtract, op1=OP.mult),
                    after=[L2, tl] + rot_y.war(ti))
                rot_y.wrote(ti, t7)
                rot_xq.read(ti, t7)
                pb = pLN[ti % 2]
                war0 = rotLN.war(ti) + ([last_ev] if last_ev else []) + rot_yT.war(gi)
                tr_last = None
                for cc in range(NCH):
                    tr_last = p.op("pe", lambda E, ss=ss, cc=cc, pb=pb: E.transpose(
                        pb[:, ts(cc, 128)],
                        ybuf[:, ss * C + cc * 128: ss * C + (cc + 1) * 128], ident[:]),
                        after=([t7, tok_ident] + war0 if cc == 0 else [t7]))
                rot_y.read(ti, tr_last)
                tev = p.op("dve", lambda E, tt=tt, gi=gi, pb=pb, yT=yT, nyT=nyT: E.tensor_copy(
                    bass.AP(yT, (gi % nyT) * NCH * QC + tt * 128,
                            [[nyT * NCH * QC, 128], [QC, NCH], [1, 128]]), pb[:]),
                    after=[tr_last])
                rotLN.wrote(ti, tev)
                last_ev = tev
            rot_yT.wrote(gi, last_ev)
            x_toks = [inf[3] for inf in tile_info]
            return last_ev, x_toks, gi % nyT

        def emit_wT(wdram, R, wT, wnat, rot_wnat, pLN, rotLN, base, extra=()):
            toks = []
            for rb in range(R // 128):
                tl = p.dma("pool", lambda E, rb=rb, wdram=wdram, base=base: E.dma_start(
                    out=wnat[:, ts(rb % 2, C)],
                    in_=wdram[base + rb * 128: base + (rb + 1) * 128, :]),
                    after=rot_wnat.war(rb) + list(extra))
                rot_wnat.wrote(rb, tl)
                pb = pLN[rb % 2]
                tr_last = None
                for cc in range(NCH):
                    tr_last = p.op("pe", lambda E, rb=rb, cc=cc, pb=pb: E.transpose(
                        pb[:, ts(cc, 128)],
                        wnat[:, (rb % 2) * C + cc * 128:(rb % 2) * C + (cc + 1) * 128],
                        ident[:]),
                        after=([tl, tok_ident] + rotLN.war(rb) if cc == 0 else [tl]))
                rot_wnat.read(rb, tr_last)
                tev = p.op("dve", lambda E, rb=rb, pb=pb, R=R, wT=wT: E.tensor_copy(
                    bass.AP(wT, rb * 128, [[NCH * R, 128], [R, NCH], [1, 128]]), pb[:]),
                    after=[tr_last])
                rotLN.wrote(rb, tev)
                toks.append(tev)
            return toks

        def emit_proj_add(sub, s_i, qc_i, wT, biasv, outf, out_dep, w_dep,
                          src, dst, PB):
            (pPRJ, rotPRJ, pTRb, rotTR, outsb, rot_outsb) = PB
            x_tok = {}
            last_prj_mm = None
            for w in range(2):
                ev_toks = {}
                for j in range(4):
                    oc = w * 4 + j
                    mi = ((s_i * NQC + qc_i) * 2 + w) * 4 + j
                    mm_last = None
                    for cc in range(NCH):
                        mm_last = p.op("pe", lambda E, oc=oc, cc=cc: E.matmul(
                            pPRJ[:], wT[:, cc * C + oc * 128: cc * C + (oc + 1) * 128],
                            outf[:, ts(cc, QC)], start=(cc == 0), stop=(cc == NCH - 1),
                            skip_group_check=True),
                            after=(rotPRJ.war(mi) + out_dep + w_dep if cc == 0 else []))
                    last_prj_mm = mm_last
                    ev = p.op("act", lambda E, oc=oc, j=j: E.activation(
                        out=outsb[:, ts(j, QC)], in_=pPRJ[:], func=AF.Identity,
                        bias=biasv[:, oc:oc + 1]),
                        after=[mm_last] + rot_outsb.war(mi) + tok_bias)
                    rotPRJ.wrote(mi, ev)
                    rot_outsb.wrote(mi, ev)
                    ev_toks[j] = (ev, mi)
                for tt in range(4):
                    ji = (s_i * NQC + qc_i) * 4 + tt
                    ii = ((s_i * NQC + qc_i) * 2 + w) * 4 + tt
                    gt = (s_i * T + qc_i * QC + tt * 128) // 128
                    r0 = gt * 128
                    bank = pTRb[tt % 2]
                    trs = []
                    for j in range(4):
                        ev, mi = ev_toks[j]
                        tr = p.op("pe", lambda E, j=j, tt=tt, bank=bank: E.transpose(
                            bank[:, ts(j, 128)],
                            outsb[:, j * QC + tt * 128: j * QC + (tt + 1) * 128],
                            ident_f[:]),
                            after=[ev, tok_identf] + (rotTR.war(ii) if j == 0 else []))
                        rot_outsb.read(mi, tr)
                        trs.append(tr)
                    if w == 0:
                        xi = 2000 * sub + ji
                        lx = p.dma("sp", lambda E, r0=r0, xi=xi: E.dma_start(
                            out=xq[:, ts(xi % 4, C)], in_=src[r0:r0 + 128, :]),
                            after=rot_xq.war(xi) + src_dep(sub, gt))
                        rot_xq.wrote(xi, lx)
                        x_tok[tt] = (lx, xi)
                    lx, xi = x_tok[tt]
                    xslot = (ji % 2) * 4 + w * 2
                    ad = p.op("dve", lambda E, ji=ji, xi=xi, w=w, bank=bank: E.tensor_add(
                        out=xnew[:, (ji % 2) * C + w * QC: (ji % 2) * C + (w + 1) * QC],
                        in0=bank[:], in1=xq[:, (xi % 4) * C + w * QC: (xi % 4) * C + (w + 1) * QC]),
                        after=trs + [lx] + rot_xnew.war(xslot) + rot_xnew.war(xslot + 1))
                    rot_xq.read(xi, ad)
                    rotTR.wrote(ii, ad)
                    sx = p.dma("sp", lambda E, r0=r0, ji=ji, w=w: E.dma_start(
                        out=dst[r0:r0 + 128, w * QC:(w + 1) * QC],
                        in_=xnew[:, (ji % 2) * C + w * QC: (ji % 2) * C + (w + 1) * QC]),
                        after=[ad])
                    rot_xnew.wrote(xslot, ad)
                    rot_xnew.wrote(xslot + 1, ad)
                    rot_xnew.read(xslot, sx)
                    rot_xnew.read(xslot + 1, sx)
                    xstore[(sub, gt, w)] = sx
            return last_prj_mm

        def attn_sublayer(sub, kind):
            src, dst = xsrc(sub), xdst(sub)
            subf = p.fence()
            wqkv = wqkv_a if kind == "a" else wqkv_b
            wproj = wproj_a if kind == "a" else wproj_b
            biasp = bias_pa if kind == "a" else bias_pb
            with ExitStack() as st:
                ee = st.enter_context
                wqkvT = ee(nc.sbuf_tensor(f"wqkvT{sub}", [128, NCH * 3 * C], BF16))
                yT = ee(nc.sbuf_tensor(f"yT{sub}", [128, 2 * NCH * QC], BF16))
                rot_yT = Rot(2)
                wprojT = ee(nc.sbuf_tensor(f"wprojT{sub}", [128, NCH * C], BF16))
                scr2 = ee(nc.sbuf_tensor(f"scr2_{sub}", [128, (2 if kind == "a" else 1) * QC], F32))
                outf = ee(nc.sbuf_tensor(f"outf{sub}", [128, NCH * QC], BF16))
                outsb = ee(nc.sbuf_tensor(f"outsb{sub}", [128, 4 * QC], F32))
                reps = ee(nc.sbuf_tensor(f"reps{sub}", [128, 2 * QC], F32))
                rrow = ee(nc.sbuf_tensor(f"rrow{sub}", [128, QC], F32R))
                rrowf = ee(nc.sbuf_tensor(f"rrowf{sub}", [128, QC], F32))
                if kind == "a":
                    scr1 = ee(nc.sbuf_tensor(f"scr1_{sub}", [128, 2 * QC], F32))
                    q_feat = ee(nc.sbuf_tensor(f"qf{sub}", [128, NCH * T], BF16))
                    kvacc = ee(nc.sbuf_tensor(f"kvacc{sub}", [128, 16 * 65], F32))
                    kvsb = ee(nc.sbuf_tensor(f"kvsb{sub}", [128, 8 * 65], BF16))
                    ktok = ee(nc.sbuf_tensor(f"ktok{sub}", [128, 2 * C], BF16))
                    vtok = ee(nc.sbuf_tensor(f"vtok{sub}", [128, 2 * H * 65], BF16))
                else:
                    vtok = ee(nc.sbuf_tensor(f"vtok{sub}", [128, NTT * H * 65], BF16))
                    expb = ee(nc.sbuf_tensor(f"expb{sub}", [128, 4 * QC], BF16))
                    avf = ee(nc.sbuf_tensor(f"avf{sub}", [128, 2 * QC], F32))
                    qstage = expb
                    kaug = ee(nc.sbuf_tensor(f"kaug{sub}", [128, 2 * T], BF16))
                    qsq = ee(nc.sbuf_tensor(f"qsq{sub}", [128, QC], BF16))
                    qaug = ee(nc.sbuf_tensor(f"qaug{sub}", [128, 2 * QC], BF16))
                    ksq = ee(nc.sbuf_tensor(f"ksq{sub}", [128, T], BF16))
                    ones_bf = ee(nc.sbuf_tensor(f"onesb{sub}", [128, 1], BF16))
                    nrm = ee(nc.sbuf_tensor(f"nrm{sub}", [128, 32], F32))

                pLNf = [ee(nc.psum_tensor(f"pLN{sub}_{i}", [128, QC], F32))
                        for i in range(2)]
                pLN = [t[:].bitcast(BF16) for t in pLNf]
                if kind == "a":
                    pMM = [ee(nc.psum_tensor(f"pMM{sub}_{i}", [128, QC], F32))
                           for i in range(2)]
                    pKV = [ee(nc.psum_tensor(f"pKV{sub}_{i}", [128, QC], F32))
                           for i in range(4)]
                    pREPs = [pKV[0], pKV[2]]
                    pPRJ = pKV[1]
                else:
                    pSC = [ee(nc.psum_tensor(f"pSC{sub}_{i}", [128, 2 * QC], F32))
                           for i in range(2)]
                    pMM = [pSC[0][:, 0:QC], pSC[1][:, 0:QC]]
                    pKV = [ee(nc.psum_tensor(f"pKV{sub}_{i}", [128, QC], F32))
                           for i in range(2)]
                    pPRJ = pKV[1]
                pTRb = [t[:] for t in pLNf]

                rotLN = Rot(2)
                rotMM = Rot(2)
                rotKV = Rot(4) if kind == "a" else Rot(2)
                rotREP = Rot(len(pREPs)) if kind == "a" else Rot(1)

                rotPRJ = Rot(1)
                rotTR = Rot(2)
                rot_outsb = Rot(4)
                rot_scr2 = Rot(2)
                rot_reps = Rot(2)
                rot_rrow = Rot(3)
                rot_outf = Rot(1)
                PB = (pPRJ, rotPRJ, pTRb, rotTR, outsb, rot_outsb)

                wnat = (q_feat if kind == "a" else ksq)
                rot_wnat = Rot(2)
                wq_toks = emit_wT(wqkv, 3 * C, wqkvT, wnat, rot_wnat, pLN, rotLN, 0,
                                  extra=subf + (tok_bias if sub == 0 else []))
                wp_toks = emit_wT(wproj, C, wprojT, wnat, rot_wnat, pLN, rotLN, 0)
                wq_done = wq_toks[-1]
                wp_done = wp_toks[-1]

                if kind == "a":
                    von = [p.op("dve", lambda E: E.memset(vtok[:], 1.0), after=subf)] * 2
                    rot_scr1 = Rot(2)
                    rot_ktok = Rot(2)
                    rot_vtok = Rot(2)
                    rot_kvacc = Rot(1)
                    rot_kvsb = Rot(1)
                else:
                    von_tok = p.op("dve", lambda E: E.memset(vtok[:], 1.0), after=subf)
                    kneg = p.op("dve", lambda E: E.memset(kaug[64:65, :], -1.0),
                                after=subf)
                    tko = p.op("dve", lambda E: E.memset(ones_bf[:], 1.0), after=subf)
                    rot_qstage = Rot(2)
                    rot_exp = Rot(2)
                    rot_vtok = Rot(NTT)
                    rot_kaug = Rot(2)
                    rot_qsq = Rot(1)
                    rotSC = Rot(2)
                    rot_avf = Rot(2)
                    rot_qaug = Rot(2)
                    rot_ksq = Rot(1)
                    rot_nrm = Rot(1)

                sample_fence = subf
                for s_i in range(S):
                    if kind == "a":
                        kvz = p.op("dve", lambda E: E.memset(kvacc[:], 0.0),
                                   after=rot_kvacc.war(s_i) + sample_fence)
                    for qc_i in range(NQC):
                        gi = s_i * NQC + qc_i
                        yT_tok, _, yT_sl = emit_ln_qchunk(
                            sub, src, s_i, qc_i, gi, pLN, rotLN, yT, 2, rot_yT,
                            extra_dep=(sample_fence if qc_i == 0 else ()))
                        if dbg and sub == 0 and gi == 0:
                            dtok = p.dma("sp", lambda E: E.dma_start(
                                out=dbg_yT, in_=yT[:, 0:NCH * QC]), after=[yT_tok])
                            rot_yT.read(gi, dtok)
                        ydep = [yT_tok, wq_done]

                        for ec in range(NCH):
                            mi = gi * 40 + ec
                            mm_last = None
                            for cc in range(NCH):
                                mm_last = p.op("pe", lambda E, ec=ec, cc=cc, mi=mi,
                                               yT_sl=yT_sl: E.matmul(
                                    pMM[mi % 2][:],
                                    wqkvT[:, cc * 3 * C + ec * 128: cc * 3 * C + (ec + 1) * 128],
                                    yT[:, yT_sl * NCH * QC + cc * QC: yT_sl * NCH * QC + (cc + 1) * QC],
                                    start=(cc == 0), stop=(cc == NCH - 1),
                                    skip_group_check=True),
                                    after=(rotMM.war(mi) + ydep if cc == 0 else []))
                            rot_yT.read(gi, mm_last)
                            qcol = ec * T + qc_i * QC
                            if kind == "a":
                                e1 = p.op("act", lambda E, mi=mi: E.activation(
                                    out=scr1[:, ts(mi % 2, QC)], in_=pMM[mi % 2][:],
                                    func=AF.Relu, scale=consts[:, 6:7]),
                                    after=[mm_last, tok_consts] + rot_scr1.war(mi))
                                e2 = p.op("act", lambda E, mi=mi: E.activation(
                                    out=scr2[:, ts(mi % 2, QC)], in_=scr1[:, ts(mi % 2, QC)],
                                    func=AF.Exp, scale=consts[:, 6:7]),
                                    after=[e1] + rot_scr2.war(mi))
                                e3 = p.op("dve", lambda E, mi=mi, qcol=qcol: E.scalar_tensor_tensor(
                                    out=q_feat[:, qcol:qcol + QC], in0=pMM[mi % 2][:],
                                    scalar=0.0, in1=scr2[:, ts(mi % 2, QC)],
                                    op0=OP.max, op1=OP.add), after=[e2])
                                rot_scr1.wrote(mi, e1)
                                rot_scr2.wrote(mi, e3)
                                rotMM.wrote(mi, e3)
                            else:
                                e3 = p.op("act", lambda E, mi=mi: E.mul(
                                    out=qstage[:, ts(mi % 2, QC)], in_=pMM[mi % 2][:],
                                    mul=SCALE), after=[mm_last] + rot_qstage.war(mi))
                                rotMM.wrote(mi, e3)
                                sq = p.dma("sp", lambda E, mi=mi, ec=ec, qc_i=qc_i: E.dma_start(
                                    out=qspill[ec * 128:(ec + 1) * 128,
                                               qc_i * QC:(qc_i + 1) * QC],
                                    in_=qstage[:, ts(mi % 2, QC)]), after=[e3])
                                rot_qstage.wrote(mi, e3)
                                rot_qstage.read(mi, sq)
                                xstore[("q", sub, s_i, ec, qc_i)] = sq

                        if kind == "a":
                            for tt in range(4):
                                ti = gi * 4 + tt
                                tcol = yT_sl * NCH * QC + tt * 128
                                kd = None
                                for eb in range(2):
                                    mi = gi * 40 + 8 + tt * 4 + eb
                                    mm_last = None
                                    for cc in range(NCH):
                                        mm_last = p.op("pe", lambda E, cc=cc, eb=eb, mi=mi,
                                                       tcol=tcol: E.matmul(
                                            pMM[mi % 2][:],
                                            yT[:, cc * QC + tcol: cc * QC + tcol + 128],
                                            wqkvT[:, cc * 3 * C + C + eb * QC: cc * 3 * C + C + (eb + 1) * QC],
                                            start=(cc == 0), stop=(cc == NCH - 1),
                                            skip_group_check=True),
                                            after=(rotMM.war(mi) + [yT_tok, wq_done]
                                                   if cc == 0 else []))
                                    rot_yT.read(gi, mm_last)
                                    e1 = p.op("act", lambda E, mi=mi: E.activation(
                                        out=scr1[:, ts(mi % 2, QC)], in_=pMM[mi % 2][:],
                                        func=AF.Relu, scale=consts[:, 6:7]),
                                        after=[mm_last, tok_consts] + rot_scr1.war(mi))
                                    e2 = p.op("act", lambda E, mi=mi: E.activation(
                                        out=scr2[:, ts(mi % 2, QC)],
                                        in_=scr1[:, ts(mi % 2, QC)], func=AF.Exp,
                                        scale=consts[:, 6:7]),
                                        after=[e1] + rot_scr2.war(mi))
                                    e3 = p.op("dve", lambda E, mi=mi, ti=ti, eb=eb: E.scalar_tensor_tensor(
                                        out=ktok[:, (ti % 2) * C + eb * QC:(ti % 2) * C + (eb + 1) * QC],
                                        in0=pMM[mi % 2][:], scalar=0.0,
                                        in1=scr2[:, ts(mi % 2, QC)], op0=OP.max, op1=OP.add),
                                        after=[e2] + (rot_ktok.war(ti) if eb == 0 else []))
                                    rot_scr1.wrote(mi, e1)
                                    rot_scr2.wrote(mi, e3)
                                    rotMM.wrote(mi, e3)
                                    kd = e3
                                rot_ktok.wrote(ti, kd)
                                vd = None
                                for eb in range(2):
                                    mi = gi * 40 + 8 + tt * 4 + 2 + eb
                                    mm_last = None
                                    for cc in range(NCH):
                                        mm_last = p.op("pe", lambda E, cc=cc, eb=eb, mi=mi,
                                                       tcol=tcol: E.matmul(
                                            pMM[mi % 2][:],
                                            yT[:, cc * QC + tcol: cc * QC + tcol + 128],
                                            wqkvT[:, cc * 3 * C + 2 * C + eb * QC: cc * 3 * C + 2 * C + (eb + 1) * QC],
                                            start=(cc == 0), stop=(cc == NCH - 1),
                                            skip_group_check=True),
                                            after=(rotMM.war(mi) + [yT_tok, wq_done]
                                                   if cc == 0 else []))
                                    rot_yT.read(gi, mm_last)
                                    ev = p.op("dve", lambda E, mi=mi, ti=ti, eb=eb: E.tensor_copy(
                                        bass.AP(vtok, (ti % 2) * H * 65 + eb * 8 * 65,
                                                [[2 * H * 65, 128], [65, 8], [1, 64]]),
                                        pMM[mi % 2][:]),
                                        after=[mm_last] + (rot_vtok.war(ti) if eb == 0 else [])
                                              + ([von[ti % 2]] if ti < 2 else []))
                                    rotMM.wrote(mi, ev)
                                    vd = ev
                                rot_vtok.wrote(ti, vd)
                                mmkv_last = {}
                                for h in range(H):
                                    slot = h % 4
                                    bk = pKV[h // 4]
                                    mmkv = p.op("pe", lambda E, h=h, bk=bk,
                                                slot=slot, ti=ti: E.matmul(
                                        bk[0:64, ts(slot, 65)],
                                        ktok[:, (ti % 2) * C + h * 64:(ti % 2) * C + (h + 1) * 64],
                                        vtok[:, (ti % 2) * H * 65 + h * 65:(ti % 2) * H * 65 + (h + 1) * 65],
                                        start=(slot == 0), stop=(slot == 3),
                                        skip_group_check=True),
                                        after=[kd, vd] + (rotKV.war(h // 4) if slot == 0 else []))
                                    rot_ktok.read(ti, mmkv)
                                    rot_vtok.read(ti, mmkv)
                                    mmkv_last[h // 4] = mmkv
                                for b in range(4):
                                    ka = p.op("dve", lambda E, b=b: E.tensor_add(
                                        out=kvacc[0:64, b * 4 * 65:(b + 1) * 4 * 65],
                                        in0=kvacc[0:64, b * 4 * 65:(b + 1) * 4 * 65],
                                        in1=pKV[b][0:64, 0:4 * 65]),
                                        after=[mmkv_last[b], kvz])
                                    rotKV.wrote(b, ka)
                        else:
                            for ec in range(NCH):
                                mi = gi * 40 + 8 + ec
                                mm_last = None
                                for cc in range(NCH):
                                    mm_last = p.op("pe", lambda E, ec=ec, cc=cc, mi=mi,
                                                   yT_sl=yT_sl: E.matmul(
                                        pMM[mi % 2][:],
                                        wqkvT[:, cc * 3 * C + C + ec * 128: cc * 3 * C + C + (ec + 1) * 128],
                                        yT[:, yT_sl * NCH * QC + cc * QC: yT_sl * NCH * QC + (cc + 1) * QC],
                                        start=(cc == 0), stop=(cc == NCH - 1),
                                        skip_group_check=True),
                                        after=(rotMM.war(mi) + ydep if cc == 0 else []))
                                rot_yT.read(gi, mm_last)
                                ekv = p.op("dve", lambda E, mi=mi: E.tensor_copy(
                                    qstage[:, ts(mi % 2, QC)], pMM[mi % 2][:]),
                                    after=[mm_last] + rot_qstage.war(mi))
                                rotMM.wrote(mi, ekv)
                                sk = p.dma("sp", lambda E, mi=mi, ec=ec, qc_i=qc_i: E.dma_start(
                                    out=kspill[ec * 128:(ec + 1) * 128, qc_i * QC:(qc_i + 1) * QC],
                                    in_=qstage[:, ts(mi % 2, QC)]), after=[ekv])
                                rot_qstage.wrote(mi, ekv)
                                rot_qstage.read(mi, sk)
                                xstore[("k", sub, s_i, ec, qc_i)] = sk
                            for tt in range(4):
                                ti = gi * 4 + tt
                                tcol = yT_sl * NCH * QC + tt * 128
                                glob_t = qc_i * 4 + tt
                                vti = s_i * NTT + glob_t
                                for eb in range(2):
                                    mi = gi * 40 + 20 + tt * 2 + eb
                                    mm_last = None
                                    for cc in range(NCH):
                                        mm_last = p.op("pe", lambda E, cc=cc, eb=eb, mi=mi,
                                                       tcol=tcol: E.matmul(
                                            pMM[mi % 2][:],
                                            yT[:, cc * QC + tcol: cc * QC + tcol + 128],
                                            wqkvT[:, cc * 3 * C + 2 * C + eb * QC: cc * 3 * C + 2 * C + (eb + 1) * QC],
                                            start=(cc == 0), stop=(cc == NCH - 1),
                                            skip_group_check=True),
                                            after=(rotMM.war(mi) + [yT_tok, wq_done]
                                                   if cc == 0 else []))
                                    rot_yT.read(gi, mm_last)
                                    ev = p.op("dve", lambda E, mi=mi, glob_t=glob_t, eb=eb: E.tensor_copy(
                                        bass.AP(vtok, glob_t * H * 65 + eb * 8 * 65,
                                                [[NTT * H * 65, 128], [65, 8], [1, 64]]),
                                        pMM[mi % 2][:]),
                                        after=[mm_last]
                                              + (rot_vtok.war(vti) if eb == 0 else [])
                                              + ([von_tok] if s_i == 0 and qc_i == 0 else []))
                                    rotMM.wrote(mi, ev)
                                    vd = ev
                                rot_vtok.wrote(vti, vd)

                    p1f = p.fence()
                    if kind == "a":
                        kv_last = [rotKV.wtok(b) for b in range(4)]
                        tkv = None
                        for h in range(H):
                            par, pp = h % 2, h // 2
                            tkv = p.op("dve", lambda E, h=h, par=par, pp=pp: E.tensor_copy(
                                kvsb[par * 64: par * 64 + 64, ts(pp, 65)],
                                kvacc[0:64, h * 65:(h + 1) * 65]),
                                after=kv_last + rot_kvsb.war(s_i))
                        rot_kvacc.wrote(s_i, tkv)
                        kv_dep = [tkv]
                        if dbg and sub == 0 and s_i == 0:
                            p.dma("sp", lambda E: E.dma_start(
                                out=dbg_kv, in_=kvacc[:]), after=[tkv])
                            dqt = p.dma("sp", lambda E: E.dma_start(
                                out=dbg_q, in_=q_feat[:]), after=p1f)
                        for qc_i in range(NQC):
                            out_toks = []
                            for h in range(H):
                                par = h % 2
                                pp = h // 2
                                mi = (s_i * NQC + qc_i) * H + h
                                qcol = pp * T + qc_i * QC
                                mo = p.op("pe", lambda E, par=par, pp=pp, mi=mi, qcol=qcol: E.matmul(
                                    pMM[mi % 2][0:65, :],
                                    kvsb[par * 64: par * 64 + 64, ts(pp, 65)],
                                    q_feat[par * 64: par * 64 + 64, qcol: qcol + QC],
                                    start=True, stop=True, skip_group_check=True),
                                    after=rotMM.war(mi) + kv_dep + p1f)
                                rot_kvsb.read(s_i, mo)
                                d1 = p.op("act", lambda E, mi=mi: E.activation(
                                    out=scr1[64:65, ts(mi % 2, QC)], in_=pMM[mi % 2][64:65, :],
                                    func=AF.Ln, bias=consts[64:65, 1:2]),
                                    after=[mo, tok_consts] + rot_scr1.war(mi))
                                d2 = p.op("act", lowp(nc, lambda E, mi=mi: E.activation(
                                    out=rrow[64:65, :], in_=scr1[64:65, ts(mi % 2, QC)],
                                    func=AF.Exp, scale=consts[64:65, 6:7])),
                                    after=[d1] + rot_rrow.war(0))
                                rot_scr1.wrote(mi, d2)
                                pR = pREPs[h % len(pREPs)]
                                rs = h % 2
                                d3 = p.op("pe", lowp(nc, lambda E, pR=pR: E.matmul(
                                    pR[0:64, :], ones_r[64:65, :], rrow[64:65, :],
                                    start=True, stop=True, skip_group_check=True)),
                                    after=[d2, tok_ones] + rotREP.war(h))
                                rot_rrow.wrote(0, d3)
                                d4 = p.op("act", lambda E, pR=pR, rs=rs: E.copy(
                                    out=reps[0:64, ts(rs, QC)], in_=pR[0:64, :]),
                                    after=[d3] + rot_reps.war(rs))
                                rotREP.wrote(h, d4)
                                rot_reps.wrote(rs, d4)
                                ocol = pp * QC
                                if par == 0:
                                    d5 = p.op("dve", lambda E, mi=mi, ocol=ocol, rs=rs: E.tensor_tensor(
                                        out=outf[0:64, ocol: ocol + QC],
                                        in0=pMM[mi % 2][0:64, :], in1=reps[0:64, ts(rs, QC)],
                                        op=OP.mult), after=[d4] + rot_outf.war(0))
                                    rotMM.wrote(mi, d5)
                                    rot_reps.read(rs, d5)
                                    out_toks.append(d5)
                                else:
                                    d5 = p.op("dve", lambda E, mi=mi, rs=rs: E.tensor_tensor(
                                        out=scr2[0:64, ts(mi % 2, QC)],
                                        in0=pMM[mi % 2][0:64, :], in1=reps[0:64, ts(rs, QC)],
                                        op=OP.mult), after=[d4] + rot_scr2.war(mi))
                                    d6 = p.op("dve", lambda E, mi=mi, ocol=ocol: E.tensor_copy(
                                        outf[64:128, ocol: ocol + QC],
                                        scr2[0:64, ts(mi % 2, QC)]),
                                        after=[d5] + rot_outf.war(0))
                                    rotMM.wrote(mi, d5)
                                    rot_scr2.wrote(mi, d6)
                                    rot_reps.read(rs, d5)
                                    out_toks.append(d6)
                            if dbg and sub == 0 and s_i == 0 and qc_i == 0:
                                dot = p.dma("sp", lambda E: E.dma_start(
                                    out=dbg_outf, in_=outf[:]), after=out_toks)
                                rot_outf.read(0, dot)
                            rot_outf.wrote(0, out_toks[-1])
                            lpm = emit_proj_add(sub, s_i, qc_i, wprojT, biasp, outf,
                                                out_toks, [wp_done], src, dst, PB)
                            rot_outf.read(0, lpm)
                    else:
                        maxk_tok = {}
                        rot_dln = Rot(1)
                        for qc_i in range(NQC):
                            out_toks = []
                            for h in range(H):
                                par = h % 2
                                pp = h // 2
                                hh = (s_i * NQC + qc_i) * H + h
                                lk = p.dma("pool", lambda E, h=h, hh=hh: E.dma_start(
                                    out=kaug[0:64, ts(hh % 2, T)],
                                    in_=kspill[h * 64:(h + 1) * 64, :]),
                                    after=rot_kaug.war(hh) + p1f
                                          + [xstore[("k", sub, s_i, pp, qq)] for qq in range(NQC)])
                                lq = p.dma("sp", lambda E, h=h, hh=hh, qc_i=qc_i: E.dma_start(
                                    out=qaug[0:64, ts(hh % 2, QC)],
                                    in_=qspill[h * 64:(h + 1) * 64, qc_i * QC:(qc_i + 1) * QC]),
                                    after=rot_qaug.war(hh) + p1f
                                          + [xstore[("q", sub, s_i, pp, qc_i)]])
                                if qc_i == 0:
                                    sqk = p.op("dve", lambda E, hh=hh: E.tensor_mul(
                                        out=ksq[0:64, :], in0=kaug[0:64, ts(hh % 2, T)],
                                        in1=kaug[0:64, ts(hh % 2, T)]),
                                        after=[lk] + rot_ksq.war(0))
                                    rds = []
                                    for j in range(4):
                                        mmn = p.op("pe", lambda E, j=j: E.matmul(
                                            pPRJ[0:1, :], ones_bf[0:64, 0:1],
                                            ksq[0:64, j * QC:(j + 1) * QC],
                                            start=True, stop=True, skip_group_check=True),
                                            after=[sqk, tko] + rotPRJ.war(0))
                                        rd = p.op("dve", lambda E, j=j: E.reduce_max(
                                            out=nrm[0:1, j:j + 1], in_=pPRJ[0:1, :],
                                            axis=mybir.AxisListType.X),
                                            after=[mmn] + rot_nrm.war(0))
                                        rotPRJ.wrote(0, rd)
                                        rds.append(rd)
                                        rot_ksq.read(0, mmn)
                                    rd2 = p.op("dve", lambda E: E.reduce_max(
                                        out=nrm[0:1, 8:9], in_=nrm[0:1, 0:4],
                                        axis=mybir.AxisListType.X), after=rds)
                                    mkl = p.op("act", lambda E: E.activation(
                                        out=nrm[0:1, 9:10], in_=nrm[0:1, 8:9],
                                        func=AF.Ln), after=[rd2])
                                    mk = p.op("act", lambda E, h=h: E.activation(
                                        out=nrm[0:1, 16 + h:17 + h], in_=nrm[0:1, 9:10],
                                        func=AF.Exp, scale=consts[0:1, 4:5]),
                                        after=[mkl, tok_consts])
                                    rot_nrm.wrote(0, mk)
                                    maxk_tok[h] = mk
                                q2 = p.op("dve", lambda E, hh=hh: E.tensor_mul(
                                    out=qsq[0:64, 0:QC], in0=qaug[0:64, ts(hh % 2, QC)],
                                    in1=qaug[0:64, ts(hh % 2, QC)]),
                                    after=[lq] + rot_qsq.war(0))
                                mmq = p.op("pe", lambda E: E.matmul(
                                    pPRJ[0:1, :], ones_bf[0:64, 0:1], qsq[0:64, 0:QC],
                                    start=True, stop=True, skip_group_check=True),
                                    after=[q2, tko] + rotPRJ.war(0))
                                rot_qsq.wrote(0, q2)
                                rot_qsq.read(0, mmq)
                                qnl = p.op("act", lambda E: E.activation(
                                    out=rrowf[0:1, :],
                                    in_=pPRJ[0:1, :], func=AF.Ln),
                                    after=[mmq] + rot_rrow.war(2))
                                qn = p.op("act", lambda E: E.activation(
                                    out=scr2[0:1, 0:QC], in_=rrowf[0:1, :], func=AF.Exp,
                                    scale=consts[0:1, 4:5]),
                                    after=[qnl, tok_consts] + rot_scr2.war(0))
                                rot_rrow.wrote(2, qn)
                                rotPRJ.wrote(0, qnl)
                                cr1 = p.op("dve", lowp(nc, lambda E, h=h: E.tensor_scalar_mul(
                                    out=rrow[0:1, :], in0=scr2[0:1, 0:QC],
                                    scalar1=nrm[0:1, 16 + h:17 + h])),
                                    after=[qn, maxk_tok[h]] + rot_rrow.war(1))
                                rot_scr2.wrote(0, cr1)
                                rot_nrm.read(0, cr1)
                                cr2 = p.op("dve", lowp(nc, lambda E, hh=hh: E.tensor_copy(
                                    qaug[64:65, ts(hh % 2, QC)], rrow[0:1, :])),
                                    after=[cr1])
                                rot_rrow.wrote(1, cr2)
                                def emit_msc(kc):
                                    pr = (kc // 2) % 2
                                    t = p.op("pe", lambda E, kc=kc, hh=hh, pr=pr: E.matmul(
                                        pSC[pr][:, (kc % 2) * QC:(kc % 2 + 1) * QC],
                                        kaug[0:65, (hh % 2) * T + kc * 128:(hh % 2) * T + (kc + 1) * 128],
                                        qaug[0:65, ts(hh % 2, QC)],
                                        start=True, stop=True, skip_group_check=True),
                                        after=(rotSC.war(kc // 2) if kc % 2 == 0 else [])
                                              + [cr2] + ([lk, kneg] if kc == 0 else []))
                                    return t

                                def emit_exp(kc2, mm1, mm2):
                                    pr = kc2 % 2
                                    ex = p.op("act", lambda E, pr=pr: E.activation(
                                        out=expb[:, ts(pr, 2 * QC)], in_=pSC[pr][:],
                                        func=AF.Exp),
                                        after=[mm1, mm2] + rot_exp.war(kc2))
                                    rotSC.wrote(kc2, ex)
                                    if dbg and s_i == 0 and qc_i == 0 and h == 0 and kc2 == 0:
                                        dex = p.dma("sp", lambda E: E.dma_start(
                                            out=dbg_expb, in_=expb[:, 0:QC]), after=[ex])
                                        rot_exp.read(kc2, dex)
                                    return ex

                                def emit_mav(kc, ex):
                                    pr = (kc // 2) % 2
                                    mav = p.op("pe", lambda E, h=h, kc=kc, pr=pr: E.matmul(
                                        pKV[0][0:65, :],
                                        vtok[:, kc * H * 65 + h * 65: kc * H * 65 + (h + 1) * 65],
                                        expb[:, pr * 2 * QC + (kc % 2) * QC: pr * 2 * QC + (kc % 2 + 1) * QC],
                                        start=(kc == 0), stop=(kc == NTT - 1),
                                        skip_group_check=True),
                                        after=[ex] + (rotKV.war(0) if kc == 0 else []))
                                    if kc % 2 == 1:
                                        rot_exp.wrote(kc // 2, mav)
                                    rot_vtok.read(s_i * NTT + kc, mav)
                                    return mav

                                mm1 = emit_msc(0)
                                mm2 = emit_msc(1)
                                ex_p = emit_exp(0, mm1, mm2)
                                av_last = None
                                for kc2 in range(1, NTT // 2):
                                    mm1 = emit_msc(2 * kc2)
                                    mm2 = emit_msc(2 * kc2 + 1)
                                    emit_mav(2 * kc2 - 2, ex_p)
                                    emit_mav(2 * kc2 - 1, ex_p)
                                    ex_p = emit_exp(kc2, mm1, mm2)
                                emit_mav(NTT - 2, ex_p)
                                av_last = emit_mav(NTT - 1, ex_p)
                                msc_p = mm2
                                rot_kaug.read(hh, msc_p)
                                rot_qaug.read(hh, msc_p)
                                rs = h % 2
                                e0 = p.op("dve", lambda E, rs=rs: E.tensor_copy(
                                    avf[0:65, ts(rs, QC)], pKV[0][0:65, :]),
                                    after=[av_last] + rot_avf.war(rs))
                                rotKV.wrote(0, e0)
                                rot_avf.wrote(rs, e0)
                                d2a = p.op("act", lambda E, rs=rs: E.activation(
                                    out=rrowf[64:65, :],
                                    in_=avf[64:65, rs * QC:(rs + 1) * QC],
                                    func=AF.Ln),
                                    after=[e0] + rot_dln.war(0))
                                d2 = p.op("act", lowp(nc, lambda E: E.activation(
                                    out=rrow[64:65, :], in_=rrowf[64:65, :],
                                    func=AF.Exp, scale=consts[64:65, 6:7])),
                                    after=[d2a, tok_consts] + rot_rrow.war(0))
                                rot_dln.wrote(0, d2)
                                rot_avf.read(rs, d2a)
                                d3 = p.op("pe", lowp(nc, lambda E: E.matmul(
                                    pSC[1][0:64, 0:QC], ones_r[64:65, :], rrow[64:65, :],
                                    start=True, stop=True, skip_group_check=True)),
                                    after=[d2, tok_ones] + rotSC.war(1))
                                rot_rrow.wrote(0, d3)
                                d4 = p.op("dve", lambda E, rs=rs: E.tensor_copy(
                                    reps[0:64, ts(rs, QC)], pSC[1][0:64, 0:QC]),
                                    after=[d3] + rot_reps.war(rs))
                                rotSC.wrote(1, d4)
                                rot_reps.wrote(rs, d4)
                                ocol = pp * QC
                                if par == 0:
                                    d5 = p.op("dve", lambda E, ocol=ocol, rs=rs: E.tensor_tensor(
                                        out=outf[0:64, ocol: ocol + QC],
                                        in0=avf[0:64, rs * QC:(rs + 1) * QC],
                                        in1=reps[0:64, ts(rs, QC)],
                                        op=OP.mult), after=[d4] + rot_outf.war(0))
                                    rot_avf.read(rs, d5)
                                    rot_reps.read(rs, d5)
                                    out_toks.append(d5)
                                else:
                                    d5 = p.op("dve", lambda E, rs=rs: E.tensor_tensor(
                                        out=scr2[0:64, 0:QC],
                                        in0=avf[0:64, rs * QC:(rs + 1) * QC],
                                        in1=reps[0:64, ts(rs, QC)],
                                        op=OP.mult), after=[d4] + rot_scr2.war(0))
                                    d6 = p.op("dve", lambda E, ocol=ocol: E.tensor_copy(
                                        outf[64:128, ocol: ocol + QC],
                                        scr2[0:64, 0:QC]),
                                        after=[d5] + rot_outf.war(0))
                                    rot_avf.read(rs, d5)
                                    rot_scr2.wrote(0, d6)
                                    rot_reps.read(rs, d5)
                                    out_toks.append(d6)
                            if dbg and s_i == 0 and qc_i == 0:
                                dob = p.dma("sp", lambda E: E.dma_start(
                                    out=dbg_outfb, in_=outf[:]), after=out_toks)
                                rot_outf.read(0, dob)
                            rot_outf.wrote(0, out_toks[-1])
                            lpm = emit_proj_add(sub, s_i, qc_i, wprojT, biasp, outf,
                                                out_toks, [wp_done], src, dst, PB)
                            rot_outf.read(0, lpm)
                    sample_fence = p.fence()

        def ffn_sublayer(sub):
            src, dst = xsrc(sub), xdst(sub)
            subf = p.fence()
            with ExitStack() as st:
                ee = st.enter_context
                w1sb = ee(nc.sbuf_tensor(f"w1sb{sub}", [128, NCH * FF], BF16))
                w2sb = ee(nc.sbuf_tensor(f"w2sb{sub}", [128, NFH * C], BF16))
                hbuf = ee(nc.sbuf_tensor(f"hbuf{sub}", [128, NFH * QC], BF16))
                yT = ee(nc.sbuf_tensor(f"yTF{sub}", [128, NCH * QC], BF16))
                rot_yT = Rot(1)
                outsb2 = ee(nc.sbuf_tensor(f"outsbF{sub}", [128, 2 * QC], F32))
                pLNf = [ee(nc.psum_tensor(f"pLNF{sub}_{i}", [128, QC], F32))
                        for i in range(2)]
                pLN = [t[:].bitcast(BF16) for t in pLNf]
                pTRb = [t[:] for t in pLNf]
                pH = [ee(nc.psum_tensor(f"pHF{sub}_{i}", [128, QC], F32)) for i in range(2)]
                pO = [ee(nc.psum_tensor(f"pOF{sub}_{i}", [128, QC], F32)) for i in range(2)]
                rotLN = Rot(2)
                rotH = Rot(2)
                rotO = Rot(2)
                rot_h = Rot(1)
                rot_outsb = Rot(2)

                w1_toks = [p.dma("pool", lambda E, cc=cc: E.dma_start(
                    out=w1sb[:, ts(cc, FF)], in_=w1d[cc * 128:(cc + 1) * 128, :]),
                    after=(subf if cc == 0 else []))
                    for cc in range(NCH)]
                w2_toks = [p.dma("pool", lambda E, hh=hh: E.dma_start(
                    out=w2sb[:, ts(hh, C)], in_=w2d[hh * 128:(hh + 1) * 128, :]))
                    for hh in range(NFH)]
                w1_done = w1_toks[-1]
                w2_done = w2_toks[-1]

                for s_i in range(S):
                    for qc_i in range(NQC):
                        gi = s_i * NQC + qc_i
                        yT_tok, x_toks, _ = emit_ln_qchunk(
                            sub, src, s_i, qc_i, gi, pLN, rotLN, yT, 1, rot_yT,
                            extra_dep=(subf if gi == 0 else ()))
                        if dbg and sub == 1 and gi == 0:
                            dt2 = p.dma("sp", lambda E: E.dma_start(
                                out=dbg_yT2, in_=yT[:]), after=[yT_tok])
                            rot_yT.read(gi, dt2)
                        h_last = None
                        first_fc_deps = rot_h.war(gi)
                        for fc in range(NFH):
                            mi = gi * 80 + fc
                            mm_last = None
                            for cc in range(NCH):
                                mm_last = p.op("pe", lambda E, fc=fc, cc=cc, mi=mi: E.matmul(
                                    pH[mi % 2][:],
                                    w1sb[:, cc * FF + fc * 128: cc * FF + (fc + 1) * 128],
                                    yT[:, ts(cc, QC)],
                                    start=(cc == 0), stop=(cc == NCH - 1),
                                    skip_group_check=True),
                                    after=(rotH.war(mi) + [yT_tok, w1_done]
                                           + (first_fc_deps if fc == 0 else [])
                                           if cc == 0 else []))
                            rot_yT.read(gi, mm_last)
                            gv = p.op("act", lambda E, fc=fc, mi=mi: E.activation(
                                out=hbuf[:, ts(fc, QC)], in_=pH[mi % 2][:], func=AF.Gelu,
                                bias=bias_1[:, fc:fc + 1]),
                                after=[mm_last] + tok_bias)
                            rotH.wrote(mi, gv)
                            h_last = gv
                        rot_h.wrote(gi, h_last)
                        if dbg and sub == 1 and gi == 0:
                            dh = p.dma("sp", lambda E: E.dma_start(
                                out=dbg_h, in_=hbuf[:]), after=[h_last])
                            rot_h.read(gi, dh)
                        ad_toks = {}
                        for w in range(4):
                            ev_toks = {}
                            for j in range(2):
                                oc = w * 2 + j
                                mi = (gi * 4 + w) * 2 + j
                                mm_last = None
                                for hh2 in range(NFH):
                                    mm_last = p.op("pe", lambda E, oc=oc, hh2=hh2, mi=mi: E.matmul(
                                        pO[mi % 2][:],
                                        w2sb[:, hh2 * C + oc * 128: hh2 * C + (oc + 1) * 128],
                                        hbuf[:, ts(hh2, QC)],
                                        start=(hh2 == 0), stop=(hh2 == NFH - 1),
                                        skip_group_check=True),
                                        after=(rotO.war(mi) + [h_last, w2_done]
                                               if hh2 == 0 else []))
                                rot_h.read(gi, mm_last)
                                ev = p.op("act", lambda E, oc=oc, j=j, mi=mi: E.activation(
                                    out=outsb2[:, ts(j, QC)], in_=pO[mi % 2][:],
                                    func=AF.Identity, bias=bias_2[:, oc:oc + 1]),
                                    after=[mm_last] + rot_outsb.war(mi) + tok_bias)
                                rotO.wrote(mi, ev)
                                rot_outsb.wrote(mi, ev)
                                if dbg and sub == 1 and gi == 0 and w == 0 and j == 1:
                                    dosb = p.dma("sp", lambda E: E.dma_start(
                                        out=dbg_osb, in_=outsb2[:]), after=[ev_toks[0][0], ev])
                                    rot_outsb.read(mi, dosb)
                                    rot_outsb.read(mi - 1, dosb)
                                ev_toks[j] = (ev, mi)
                            for tt in range(4):
                                ji = gi * 4 + tt
                                lnidx = gi * 16 + w * 4 + tt
                                bank = pTRb[tt % 2]
                                trs = []
                                for j in range(2):
                                    ev, mi = ev_toks[j]
                                    tr = p.op("pe", lambda E, j=j, tt=tt, bank=bank: E.transpose(
                                        bank[:, ts(j, 128)],
                                        outsb2[:, j * QC + tt * 128: j * QC + (tt + 1) * 128],
                                        ident_f[:]),
                                        after=[ev, tok_identf]
                                              + (rotLN.war(lnidx) if j == 0 else []))
                                    rot_outsb.read(mi, tr)
                                    trs.append(tr)
                                xi = gi * 4 + tt
                                xslot = (ji % 2) * 4 + w
                                gt = (s_i * T + qc_i * QC + tt * 128) // 128
                                r0 = gt * 128
                                ad = p.op("dve", lambda E, ji=ji, xi=xi, w=w, bank=bank: E.tensor_add(
                                    out=xnew[:, (ji % 2) * C + w * 256: (ji % 2) * C + (w + 1) * 256],
                                    in0=bank[:, 0:256],
                                    in1=xq[:, (xi % 4) * C + w * 256: (xi % 4) * C + (w + 1) * 256]),
                                    after=trs + [x_toks[tt]] + rot_xnew.war(xslot))
                                rot_xq.read(xi, ad)
                                rotLN.wrote(lnidx, ad)
                                rot_xnew.wrote(xslot, ad)
                                sx = p.dma("sp", lambda E, r0=r0, ji=ji, w=w: E.dma_start(
                                    out=dst[r0:r0 + 128, w * 256:(w + 1) * 256],
                                    in_=xnew[:, (ji % 2) * C + w * 256: (ji % 2) * C + (w + 1) * 256]),
                                    after=[ad])
                                rot_xnew.read(xslot, sx)
                                xstore[(sub, gt, "f%d" % w)] = sx

        seq = ["a", "f", "b", "f"]
        for sub in range(n_sub):
            if seq[sub] == "f":
                ffn_sublayer(sub)
            else:
                attn_sublayer(sub, seq[sub])

        with nc.Block() as block:
            p.emit(nc, block, stack)

    return nc


def kernel(L0_attn_a_x, L0_attn_a_Wqkv, L0_attn_a_Wproj, L0_attn_a_bproj,
           L0_attn_b_w_qkv, L0_attn_b_w_proj, L0_attn_b_b_proj,
           L0_ffn_w1, L0_ffn_b1, L0_ffn_w2, L0_ffn_b2, n_sub=4):
    x = np.ascontiguousarray(np.asarray(L0_attn_a_x, dtype=np.float32))
    B, T_, C_ = x.shape
    n = 8
    nc = build_program(n_sub=n_sub)
    ident = np.eye(128, dtype=ml_dtypes.bfloat16)
    base = {
        "wqkv_a": np.ascontiguousarray(np.asarray(L0_attn_a_Wqkv, dtype=np.float32)),
        "wproj_a": np.ascontiguousarray(np.asarray(L0_attn_a_Wproj, dtype=np.float32)),
        "bproj_a": np.ascontiguousarray(np.asarray(L0_attn_a_bproj, dtype=np.float32)),
        "wqkv_b": np.ascontiguousarray(np.asarray(L0_attn_b_w_qkv, dtype=np.float32)),
        "wproj_b": np.ascontiguousarray(np.asarray(L0_attn_b_w_proj, dtype=np.float32)),
        "bproj_b": np.ascontiguousarray(np.asarray(L0_attn_b_b_proj, dtype=np.float32)),
        "w1": np.ascontiguousarray(np.asarray(L0_ffn_w1, dtype=np.float32)),
        "b1": np.ascontiguousarray(np.asarray(L0_ffn_b1, dtype=np.float32)),
        "w2": np.ascontiguousarray(np.asarray(L0_ffn_w2, dtype=np.float32)),
        "b2": np.ascontiguousarray(np.asarray(L0_ffn_b2, dtype=np.float32)),
        "ident": ident,
    }
    in_maps = []
    for c in range(n):
        m = dict(base)
        m["xin"] = np.ascontiguousarray(x[c * S:(c + 1) * S].reshape(S * T_, C_))
        in_maps.append(m)
    res = run_bass_kernel_spmd(nc, in_maps, core_ids=list(range(n)))
    out = np.empty((B, T_, C_), dtype=np.float32)
    for c in range(n):
        out[c * S:(c + 1) * S] = np.asarray(res.results[c]["xout"]).reshape(S, T_, C_)
    return out
